# version 22 (final)
# baseline (speedup 1.0000x reference)
import contextlib
import numpy as np
import ml_dtypes
import concourse.bass as bass
import concourse.mybir as mybir
from concourse.bass_utils import run_bass_kernel_spmd

F32 = mybir.dt.float32
BF16 = mybir.dt.bfloat16
AF = mybir.ActivationFunctionType
OP = mybir.AluOpType
AX = mybir.AxisListType

S = 16384
D = 1024
TC = 4096
NEG = -3.0e38
EPS = 1e-5


class Ctr:
    __slots__ = ("h", "n")
    def __init__(self, h):
        self.h = h
        self.n = 0
    def inc(self, ins, k=1):
        ins.then_inc(self.h, k)
        self.n += k
        return self.n


def build_program(debug=False):
    nc = bass.Bass("TRN2", target_bir_lowering=False)
    scratch_kind = "ExternalOutput" if debug else "Internal"

    xk = nc.dram_tensor("xk", [S, D], F32, kind="ExternalInput").ap()
    xq = nc.dram_tensor("xq", [TC, D], F32, kind="ExternalInput").ap()
    wkv = nc.dram_tensor("wkv", [D, 2048], F32, kind="ExternalInput").ap()
    wq = nc.dram_tensor("wq", [D, D], F32, kind="ExternalInput").ap()
    gwT = nc.dram_tensor("gwT", [D, 32], F32, kind="ExternalInput").ap()
    gbr = nc.dram_tensor("gbr", [128, 32], F32, kind="ExternalInput").ap()
    nsc = nc.dram_tensor("nsc", [128, 8], F32, kind="ExternalInput").ap()
    b1g = nc.dram_tensor("b1g", [128, 256], F32, kind="ExternalInput").ap()
    b1l = nc.dram_tensor("b1l", [128, 256], F32, kind="ExternalInput").ap()
    b2m = nc.dram_tensor("b2m", [32, 1024], F32, kind="ExternalInput").ap()
    W1H = nc.dram_tensor("W1H", [32, 128, 16384], F32, kind="ExternalInput").ap()
    W2H = nc.dram_tensor("W2H", [32, 128, 8192], F32, kind="ExternalInput").ap()
    masks = nc.dram_tensor("masks", [8, 128, 128], F32, kind="ExternalInput").ap()
    trim = nc.dram_tensor("trim", [128, 1024], BF16, kind="ExternalInput").ap()
    onesr = nc.dram_tensor("onesr", [1, 128], BF16, kind="ExternalInput").ap()
    identf = nc.dram_tensor("identf", [128, 128], F32, kind="ExternalInput").ap()
    onescol = nc.dram_tensor("onescol", [128, 1], BF16, kind="ExternalInput").ap()
    out = nc.dram_tensor("out", [TC, D], F32, kind="ExternalOutput").ap()

    xln_sc = nc.dram_tensor("xln_sc", [S, D], BF16, kind=scratch_kind).ap()
    xlnq_sc = nc.dram_tensor("xlnq_sc", [TC, D], BF16, kind=scratch_kind).ap()
    kT_sc = nc.dram_tensor("kT_sc", [16, 128, 8192], BF16, kind=scratch_kind).ap()
    v_sc = nc.dram_tensor("v_sc", [16, 128, 8192], BF16, kind=scratch_kind).ap()
    qT_sc = nc.dram_tensor("qT_sc", [D, TC], BF16, kind=scratch_kind).ap()
    x1_sc = nc.dram_tensor("x1_sc", [TC, D], F32, kind=scratch_kind).ap()
    t2_sc = nc.dram_tensor("t2_sc", [TC, D], BF16, kind=scratch_kind).ap()

    w1A = nc.alloc_sbuf_tensor("w1A", [128, 16384], BF16)
    w1B = nc.alloc_sbuf_tensor("w1B", [128, 16384], BF16)
    w2A = nc.alloc_sbuf_tensor("w2A", [128, 8192], BF16)
    w2B = nc.alloc_sbuf_tensor("w2B", [128, 8192], BF16)
    t2T = nc.alloc_sbuf_tensor("t2T", [128, 8192], BF16)
    hwT = nc.alloc_sbuf_tensor("hwT", [128, 8192], BF16)
    oacc = nc.alloc_sbuf_tensor("oacc", [128, 8192], F32)

    masks_sb = nc.alloc_sbuf_tensor("masks_sb", [128, 1024], F32)
    tri_sb = nc.alloc_sbuf_tensor("tri_sb", [128, 1024], BF16)
    gas = [nc.alloc_sbuf_tensor(f"gas{i}", [128, 512], BF16) for i in range(2)]
    gmin = [nc.alloc_sbuf_tensor(f"gmin{i}", [128, 512], F32) for i in range(2)]
    lc1 = nc.alloc_sbuf_tensor("lc1", [128, 512], F32)
    lc2 = [nc.alloc_sbuf_tensor(f"lc2{i}", [128, 512], BF16) for i in range(2)]
    b1gb = nc.alloc_sbuf_tensor("b1gb", [128, 256], F32)
    b1lb = nc.alloc_sbuf_tensor("b1lb", [128, 256], F32)
    gwT_sb = nc.alloc_sbuf_tensor("gwT_sb", [128, 256], BF16)
    gbrep = nc.alloc_sbuf_tensor("gbrep", [128, 32], F32)
    nsc_sb = nc.alloc_sbuf_tensor("nsc_sb", [128, 8], F32)
    idf_sb = nc.alloc_sbuf_tensor("idf_sb", [128, 128], F32)
    b2_sb = nc.alloc_sbuf_tensor("b2_sb", [32, 1024], BF16)
    cmbT = nc.alloc_sbuf_tensor("cmbT", [32, 1024], BF16)
    onecb = nc.alloc_sbuf_tensor("onecb", [128, 1], BF16)
    g_sb = nc.alloc_sbuf_tensor("g_sb", [128, 256], F32)
    me_sb = nc.alloc_sbuf_tensor("me_sb", [128, 256], F32)
    msk_sb = nc.alloc_sbuf_tensor("msk_sb", [128, 256], F32)
    rnkf = nc.alloc_sbuf_tensor("rnkf", [128, 256], F32)
    cmb = nc.alloc_sbuf_tensor("cmb", [128, 256], F32)
    nmx = nc.alloc_sbuf_tensor("nmx", [128, 8], F32)
    zr = nc.alloc_sbuf_tensor("zr", [128, 8], F32)
    izr = nc.alloc_sbuf_tensor("izr", [128, 8], F32)
    stt = nc.alloc_sbuf_tensor("stt", [128, 32], F32)
    pw_sb = nc.alloc_sbuf_tensor("pw_sb", [128, 8192], BF16)

    ST = {n: stt[:, i:i + 1] for i, n in enumerate(
        ["ssum", "sqs", "negm", "m2", "vpe", "vpe2", "lnv", "rs",
         "iv", "vrpe", "lnv2", "r2", "sstar", "lrec0", "lrec1"])}

    Wkv_sb = w1A
    Wq_sb = w2B
    xlnT = [w2A[:, 0:4096], w2A[:, 4096:8192]]
    kTsb = [t2T[:, 2560 + 512 * i: 3072 + 512 * i] for i in range(2)]
    vsb = [t2T[:, 3584 + 1024 * i: 4608 + 1024 * i] for i in range(2)]
    xln_tm = [hwT[:, 1024 * i: 1024 * (i + 1)] for i in range(2)]
    xtm = [oacc[:, 1024 * i: 1024 * (i + 1)] for i in range(2)]
    junk = oacc[:, 5120:6144]
    kTst = [w1A[:, 0:8192], w1A[:, 8192:16384]]
    vst = [w1B[:, 0:8192], w1B[:, 8192:16384]]
    qTsb = [hwT[:, 2048:4096], hwT[:, 4096:6144]]
    PT = [t2T[:, 0:256], t2T[:, 256:512], t2T[:, 2560:2816]]
    t2tm = [t2T[:, 512 + 1024 * i: 1536 + 1024 * i] for i in range(2)]
    xqb = [oacc[:, 1024 + 1024 * i: 2048 + 1024 * i] for i in range(2)]
    x1b = [oacc[:, 3072 + 1024 * i: 4096 + 1024 * i] for i in range(2)]
    pw = pw_sb

    PS = nc.alloc_psum_tensor("PS", [128, 4096], F32)
    def bank(i, w=512):
        return PS[:, 512 * i: 512 * i + w]

    stack = contextlib.ExitStack()
    sem_names = ["dMISC", "dXK", "dXLN", "dXLT", "dKT", "dVV", "dQT", "dKTS",
                 "dVS", "dQS", "dXQ", "dX1", "dT2", "dT2L", "dW", "dOA",
                 "dOUT", "tsm", "vsm", "ssm", "vch", "sch"]
    SM = {n: Ctr(stack.enter_context(nc.semaphore(n))) for n in sem_names}
    block = stack.enter_context(nc.Block())

    @block.gpsimd
    def _(gp):
        T = nc.tensor; V = nc.vector; Sc = nc.scalar; Sy = nc.sync; G = nc.gpsimd
        tsm, vsm, ssm, vch, sch = (SM[k] for k in ["tsm", "vsm", "ssm", "vch", "sch"])
        dMISC, dXK, dXLN, dXLT = (SM[k] for k in ["dMISC", "dXK", "dXLN", "dXLT"])
        dKT, dVV, dQT, dKTS, dVS, dQS = (SM[k] for k in
                                         ["dKT", "dVV", "dQT", "dKTS", "dVS", "dQS"])
        dXQ, dX1, dT2, dT2L, dW, dOA, dOUT = (SM[k] for k in
            ["dXQ", "dX1", "dT2", "dT2L", "dW", "dOA", "dOUT"])

        def fm_src(dram_ap, n):
            return bass.AP(dram_ap.tensor, 0, [[n, 128], [128 * n, 8], [1, n]])
        dMISC.inc(G.dma_start(Wkv_sb[:, :].rearrange("p (c n) -> p c n", c=8),
                              fm_src(wkv, 2048)), 16)
        dMISC.inc(G.dma_start(Wq_sb[:, 0:8192].rearrange("p (c n) -> p c n", c=8),
                              fm_src(wq, 1024)), 16)
        dMISC.inc(G.dma_start(gwT_sb[:, :].rearrange("p (c n) -> p c n", c=8),
                              fm_src(gwT, 32)), 16)
        dMISC.inc(G.dma_start(gbrep[:, :], gbr), 16)
        dMISC.inc(G.dma_start(nsc_sb[:, :], nsc), 16)
        dMISC.inc(G.dma_start(b1gb[:, :], b1g), 16)
        dMISC.inc(G.dma_start(b1lb[:, :], b1l), 16)
        dMISC.inc(G.dma_start(onecb[:, :], onescol), 16)
        dMISC.inc(G.dma_start(
            masks_sb[:, :].rearrange("p (j r) -> p j r", j=8),
            bass.AP(masks.tensor, 0, [[128, 128], [128 * 128, 8], [1, 128]])), 16)
        dMISC.inc(G.dma_start(tri_sb[:, :], trim), 16)
        dMISC.inc(G.dma_start(idf_sb[:, :], identf), 16)
        dMISC.inc(G.dma_start(b2_sb[:, :], b2m), 16)
        setup_cnt = dMISC.n

        xtm8 = [oacc[:, 1024 * i: 1024 * (i + 1)] for i in range(8)]
        xln8 = [hwT[:, 1024 * i: 1024 * (i + 1)] for i in range(8)]
        stt4 = nc.alloc_sbuf_tensor("stt4x", [128, 64], F32)
        def st4(par, name):
            i = ["ssum", "sqs", "negm", "m2", "vpe", "vpe2", "lnv", "rs"].index(name)
            return stt4[:, 32 * par + 4 * i: 32 * par + 4 * (i + 1)]

        ln_prev = {"vx": [None] * 8, "sq": [None] * 8, "wr": [None] * 8,
                   "c4": [0, 0]}
        mm_end = {}
        ln_tasks = [("kv", xk, sg, SM["dXK"], xln_sc) for sg in range(32)] + \
                   [("q", xq, sg, SM["dXQ"], xlnq_sc) for sg in range(8)]

        def ln_super(task_i):
            _, src_dram, sg, xd_ctr, sc_dram = ln_tasks[task_i]
            par = task_i % 2
            for tau in range(4):
                slot = 4 * par + tau
                if ln_prev["vx"][slot] is not None:
                    Sy.wait_ge(vsm.h, ln_prev["vx"][slot])
                    Sy.wait_ge(ssm.h, ln_prev["sq"][slot])
                t = 4 * sg + tau
                ins = Sy.dma_start(xtm8[slot][:, :],
                                   src_dram[128 * t:128 * (t + 1), :])
                xd_ctr.inc(ins, 16)
            ld_c = xd_ctr.n
            V.wait_ge(xd_ctr.h, ld_c)
            V.wait_ge(vch.h, ln_prev["c4"][par])
            r4 = vch.inc(V.tensor_reduce(
                st4(par, "ssum"),
                bass.AP(oacc, 4096 * par, [[8192, 128], [1024, 4], [1, 1024]]),
                AX.X, OP.add))
            sq_c = [None] * 4
            for tau in range(4):
                slot = 4 * par + tau
                Sc.wait_ge(xd_ctr.h, ld_c)
                Sc.wait_ge(vch.h, ln_prev["c4"][par])
                if ln_prev["vx"][slot] is not None:
                    Sc.wait_ge(vsm.h, ln_prev["vx"][slot])
                if ln_prev["wr"][slot] is not None:
                    Sc.wait_ge(SM["dXLN"].h, ln_prev["wr"][slot])
                sq_c[tau] = ssm.inc(Sc.activation(
                    xln8[slot][:, :], xtm8[slot][:, :], AF.Square,
                    accum_out=st4(par, "sqs")[:, tau:tau + 1]))
            V.wait_ge(vch.h, r4)
            c1 = vch.inc(V.tensor_scalar_mul(st4(par, "negm"), st4(par, "ssum"),
                                             -1.0 / 1024.0))
            V.wait_ge(vch.h, c1)
            c2 = vch.inc(V.tensor_tensor(st4(par, "m2"), st4(par, "negm"),
                                         st4(par, "negm"), OP.mult))
            V.wait_ge(ssm.h, sq_c[3])
            c3 = vch.inc(V.tensor_scalar(st4(par, "vpe"), st4(par, "sqs"),
                                         1.0 / 1024.0, EPS, OP.mult, OP.add))
            V.wait_ge(vch.h, c3)
            c4 = vch.inc(V.tensor_tensor(st4(par, "vpe2"), st4(par, "vpe"),
                                         st4(par, "m2"), OP.subtract))
            Sc.wait_ge(vch.h, c4)
            s1 = sch.inc(Sc.activation(st4(par, "lnv"), st4(par, "vpe2"), AF.Ln))
            Sc.wait_ge(sch.h, s1)
            s2 = ssm.inc(Sc.activation(st4(par, "rs"), st4(par, "lnv"), AF.Exp,
                                       scale=-0.5))
            for tau in range(4):
                slot = 4 * par + tau
                t = 4 * sg + tau
                V.wait_ge(ssm.h, s2)
                if ln_prev["wr"][slot] is not None:
                    V.wait_ge(SM["dXLN"].h, ln_prev["wr"][slot])
                vx = vsm.inc(V.tensor_scalar(
                    xln8[slot][:, :], xtm8[slot][:, :],
                    st4(par, "negm")[:, tau:tau + 1],
                    st4(par, "rs")[:, tau:tau + 1],
                    OP.add, OP.mult))
                ln_prev["vx"][slot] = vx
                ln_prev["sq"][slot] = sq_c[tau]
                Sy.wait_ge(vsm.h, vx)
                ins = Sy.dma_start(sc_dram[128 * t:128 * (t + 1), :],
                                   xln8[slot][:, :])
                SM["dXLN"].inc(ins, 16)
                ln_prev["wr"][slot] = SM["dXLN"].n
            ln_prev["c4"][par] = c4
            return SM["dXLN"].n

        ln_wr = {}
        xlt_c = {}

        def ln_transposes(task_i):
            _, src_dram, sg, xd_ctr, sc_dram = ln_tasks[task_i]
            Sy.wait_ge(SM["dXLN"].h, ln_wr[task_i])
            if task_i >= 2:
                Sy.wait_ge(tsm.h, mm_end[task_i - 2])
            for dc in range(8):
                ins = Sy.dma_start(
                    xlnT[task_i % 2][:, 512 * dc: 512 * (dc + 1)],
                    sc_dram[512 * sg: 512 * (sg + 1),
                            128 * dc: 128 * (dc + 1)],
                    transpose=True)
                SM["dXLT"].inc(ins, 16)
            xlt_c[task_i] = SM["dXLT"].n

        kT_wr = {0: None, 1: None}
        v_wr = {0: None, 1: None}
        vevac_roll = {0: None, 1: None}
        kevac_roll = {0: None, 1: None}

        def kv_mms(task_i, sg):
            T.wait_ge(SM["dXLT"].h, xlt_c[task_i])
            if task_i == 0:
                T.wait_ge(dMISC.h, setup_cnt)
            for dout in range(8):
                pb = dout % 2
                if kevac_roll[pb] is not None:
                    T.wait_ge(vsm.h, kevac_roll[pb])
                for dcin in range(8):
                    lhs = Wkv_sb[:, 2048 * dcin + 128 * dout:
                                 2048 * dcin + 128 * dout + 128]
                    rhs = xlnT[task_i % 2][:, 512 * dcin: 512 * (dcin + 1)]
                    ins = T.matmul(bank(pb), lhs, rhs,
                                   start=(dcin == 0), stop=(dcin == 7))
                tmm = tsm.inc(ins)
                V.wait_ge(tsm.h, tmm)
                if kT_wr[pb] is not None:
                    V.wait_ge(kT_wr[pb][0].h, kT_wr[pb][1])
                kevac_roll[pb] = vsm.inc(V.tensor_copy(kTsb[pb][:, :], bank(pb)))
                Sc.wait_ge(vsm.h, kevac_roll[pb])
                ins = Sc.dma_start(
                    bass.AP(kT_sc.tensor,
                            (sg // 2) * 128 * 8192 + 1024 * dout + 512 * (sg % 2),
                            [[8192, 128], [1, 512]]),
                    kTsb[pb][:, :])
                dKT.inc(ins, 16)
                kT_wr[pb] = (dKT, dKT.n)
            for tau in range(4):
                pb = tau % 2
                if vevac_roll[pb] is not None:
                    T.wait_ge(vsm.h, vevac_roll[pb])
                for dcin in range(8):
                    lhs = xlnT[task_i % 2][:, 512 * dcin + 128 * tau:
                                           512 * dcin + 128 * tau + 128]
                    for hh in range(2):
                        rhs = Wkv_sb[:, 2048 * dcin + 1024 + 512 * hh:
                                     2048 * dcin + 1024 + 512 * (hh + 1)]
                        ins = T.matmul(PS[:, 2048 + 1024 * pb + 512 * hh:
                                          2048 + 1024 * pb + 512 * (hh + 1)],
                                       lhs, rhs, start=(dcin == 0),
                                       stop=(dcin == 7))
                tmm = tsm.inc(ins)
                V.wait_ge(tsm.h, tmm)
                if v_wr[pb] is not None:
                    V.wait_ge(dVV.h, v_wr[pb])
                vevac_roll[pb] = vsm.inc(
                    V.tensor_copy(vsb[pb][:, :],
                                  PS[:, 2048 + 1024 * pb: 3072 + 1024 * pb]))
                t = 4 * sg + tau
                Sc.wait_ge(vsm.h, vevac_roll[pb])
                ins = Sc.dma_start(
                    bass.AP(v_sc.tensor, (t // 8) * 128 * 8192 + 1024 * (t % 8),
                            [[8192, 128], [1, 1024]]),
                    vsb[pb][:, :])
                dVV.inc(ins, 16)
                v_wr[pb] = dVV.n
            mm_end[task_i] = tsm.n

        def q_mms(task_i, sg):
            T.wait_ge(SM["dXLT"].h, xlt_c[task_i])
            for dout in range(8):
                pb = dout % 2
                if kevac_roll[pb] is not None:
                    T.wait_ge(vsm.h, kevac_roll[pb])
                for dcin in range(8):
                    lhs = Wq_sb[:, 1024 * dcin + 128 * dout:
                                1024 * dcin + 128 * dout + 128]
                    rhs = xlnT[task_i % 2][:, 512 * dcin: 512 * (dcin + 1)]
                    ins = T.matmul(bank(pb), lhs, rhs,
                                   start=(dcin == 0), stop=(dcin == 7))
                tmm = tsm.inc(ins)
                V.wait_ge(tsm.h, tmm)
                if kT_wr[pb] is not None:
                    V.wait_ge(kT_wr[pb][0].h, kT_wr[pb][1])
                kevac_roll[pb] = vsm.inc(
                    V.tensor_scalar_mul(kTsb[pb][:, :], bank(pb), 1.0 / 32.0))
                Sc.wait_ge(vsm.h, kevac_roll[pb])
                ins = Sc.dma_start(
                    qT_sc[128 * dout:128 * (dout + 1), 512 * sg: 512 * (sg + 1)],
                    kTsb[pb][:, :])
                dQT.inc(ins, 16)
                kT_wr[pb] = (dQT, dQT.n)
            mm_end[task_i] = tsm.n

        ln_wr[0] = ln_super(0)
        ln_wr[1] = ln_super(1)
        ln_transposes(0)
        for i in range(40):
            if i + 1 < 40:
                ln_transposes(i + 1)
            if ln_tasks[i][0] == "kv":
                kv_mms(i, ln_tasks[i][2])
            else:
                q_mms(i, ln_tasks[i][2])
            if i + 2 < 40:
                ln_wr[i + 2] = ln_super(i + 2)

        stage1_end = {k: SM[k].n for k in sem_names}

        for eng in (V, Sc, Sy, G):
            for kk in ("dXLN", "dKT", "dQT", "dVV", "dXK", "dXQ", "dXLT"):
                eng.wait_ge(SM[kk].h, stage1_end[kk])
            eng.wait_ge(vsm.h, stage1_end["vsm"])
            eng.wait_ge(ssm.h, stage1_end["ssm"])

        def span_load(sigma, buf, guard_t):
            Sy.wait_ge(tsm.h, guard_t)
            Sc.wait_ge(tsm.h, guard_t)
            src_k = bass.AP(kT_sc.tensor, sigma * 128 * 8192,
                            [[8192, 128], [1, 8192]])
            ins = Sy.dma_start(kTst[buf][:, :], src_k)
            SM["dKTS"].inc(ins, 16)
            src_v = bass.AP(v_sc.tensor, sigma * 128 * 8192,
                            [[8192, 128], [1, 8192]])
            ins = Sc.dma_start(vst[buf][:, :], src_v)
            SM["dVS"].inc(ins, 16)
            return SM["dKTS"].n, SM["dVS"].n

        fin_guard = {}
        qT_guard = {}
        spanbuf_lastuse = {}
        prev_super_lastAV = stage1_end["tsm"]
        prev_fin_v = None
        prev_fin_c0 = None
        pending_fin = None
        for a in range(16):
            ntrip = 8 * a + 8
            nspan = a + 1
            G.wait_ge(tsm.h, qT_guard[a - 2] if a >= 2 else stage1_end["tsm"])
            if a == 0:
                G.wait_ge(SM["dQT"].h, stage1_end["dQT"])
            src_q = bass.AP(qT_sc.tensor, 256 * a,
                            [[TC, 128], [128 * TC, 8], [1, 256]])
            ins = G.dma_start(qTsb[a % 2][:, :].rearrange("p (c n) -> p c n", c=8),
                              src_q)
            SM["dQS"].inc(ins, 16)
            qs_c = SM["dQS"].n
            span_c = {}
            for sigma in range(min(2, nspan)):
                span_c[sigma] = span_load(sigma, sigma % 2,
                                          spanbuf_lastuse.get(sigma % 2,
                                                              prev_super_lastAV))
            sc_c = {}
            exp_c = {}
            avt_c = {}
            mask_c = {}
            SB = (0, 1, 3)

            def emit_scores(kappa, a=a, ntrip=ntrip, span_c=span_c, sc_c=sc_c,
                            exp_c=exp_c, avt_c=avt_c, qs_c=qs_c,
                            psl=prev_super_lastAV):
                sigma = kappa // 8
                sub8 = kappa % 8
                sbuf = sigma % 2
                cb = kappa % 3
                T.wait_ge(SM["dKTS"].h, span_c[sigma][0])
                if kappa == 0:
                    T.wait_ge(SM["dQS"].h, qs_c)
                    if a == 0:
                        T.wait_ge(dMISC.h, setup_cnt)
                if kappa >= 3:
                    T.wait_ge(ssm.h, exp_c[kappa - 3])
                for dc in range(8):
                    lhs = kTst[sbuf][:, 1024 * dc + 128 * sub8:
                                     1024 * dc + 128 * sub8 + 128]
                    rhs = qTsb[a % 2][:, 256 * dc: 256 * (dc + 1)]
                    ins = T.matmul(bank(SB[cb], 256), lhs, rhs,
                                   start=(dc == 0), stop=(dc == 7))
                sc_c[kappa] = tsm.inc(ins)
                mask_cnt = None
                j0 = kappa - 8 * a
                if j0 >= 0:
                    for m in range(2):
                        j = j0 - 4 * m
                        if j >= 0:
                            V.wait_ge(tsm.h, sc_c[kappa])
                            ins = V.tensor_tensor(
                                bank(SB[cb], 256)[:, 128 * m:128 * (m + 1)],
                                bank(SB[cb], 256)[:, 128 * m:128 * (m + 1)],
                                masks_sb[:, 128 * j:128 * (j + 1)], OP.add)
                            mask_cnt = vsm.inc(ins)
                mask_c[kappa] = mask_cnt

            def emit_exp(kappa, ntrip=ntrip, sc_c=sc_c, exp_c=exp_c,
                         avt_c=avt_c, mask_c=mask_c, psl=prev_super_lastAV):
                cb = kappa % 3
                if mask_c.get(kappa) is not None:
                    Sc.wait_ge(vsm.h, mask_c[kappa])
                else:
                    Sc.wait_ge(tsm.h, sc_c[kappa])
                Sc.wait_ge(tsm.h, avt_c[kappa - 3] if kappa >= 3 else psl)
                ins = Sc.activation(PT[cb][:, :], bank(SB[cb], 256), AF.Exp)
                exp_c[kappa] = ssm.inc(ins)

            def emit_av(kappa, a=a, ntrip=ntrip, nspan=nspan, span_c=span_c,
                        exp_c=exp_c, avt_c=avt_c):
                sigma = kappa // 8
                sub8 = kappa % 8
                sbuf = sigma % 2
                cb = kappa % 3
                T.wait_ge(ssm.h, exp_c[kappa])
                T.wait_ge(SM["dVS"].h, span_c[sigma][1])
                if kappa == 0 and prev_fin_v is not None:
                    T.wait_ge(vsm.h, prev_fin_v)
                    T.wait_ge(vch.h, prev_fin_c0)
                for m in range(2):
                    for hh in range(2):
                        T.matmul(
                            PS[:, 2048 + 1024 * m + 512 * hh:
                               2048 + 1024 * m + 512 * (hh + 1)],
                            PT[cb][:, 128 * m:128 * (m + 1)],
                            vst[sbuf][:, 1024 * sub8 + 512 * hh:
                                      1024 * sub8 + 512 * (hh + 1)],
                            start=(kappa == 0), stop=(kappa == ntrip - 1))
                    ins = T.matmul(PS[:, 1024 + m:1025 + m],
                                   PT[cb][:, 128 * m:128 * (m + 1)],
                                   onecb[:, :],
                                   start=(kappa == 0 and m == 0),
                                   stop=(kappa == ntrip - 1 and m == 1))
                avt_c[kappa] = tsm.inc(ins)
                if sub8 == 7:
                    spanbuf_lastuse[sigma % 2] = avt_c[kappa]
                    if sigma + 2 < nspan:
                        span_c[sigma + 2] = span_load(sigma + 2, sigma % 2,
                                                      avt_c[kappa])

            emit_scores(0)
            emit_scores(1)
            emit_exp(0)
            if pending_fin is not None:
                pending_fin()
                pending_fin = None
            for kappa in range(ntrip):
                if kappa + 2 < ntrip:
                    emit_scores(kappa + 2)
                if kappa + 1 < ntrip:
                    emit_exp(kappa + 1)
                emit_av(kappa)
            for m in range(2):
                g = fin_guard.get((a - 1, m))
                if g is not None:
                    Sy.wait_ge(SM["dX1"].h, g["x1_wr"])
                ins = Sy.dma_start(xqb[m][:, :],
                                   xq[256 * a + 128 * m: 256 * a + 128 * (m + 1), :])
                SM["dXQ"].inc(ins, 16)
            xq_c = SM["dXQ"].n
            prev_super_lastAV = avt_c[ntrip - 1]
            qT_guard[a] = sc_c[ntrip - 1]

            def make_fin(a=a, lastAV=avt_c[ntrip - 1], xq_c=xq_c):
                def fin():
                    nonlocal prev_fin_v, prev_fin_c0
                    V.wait_ge(tsm.h, lastAV)
                    c0 = vch.inc(V.reciprocal(stt[:, 13:15], PS[:, 1024:1026]))
                    prev_fin_c0 = c0
                    x1cs = {}
                    V.wait_ge(SM["dXQ"].h, xq_c)
                    V.wait_ge(vch.h, c0)
                    for m in range(2):
                        g = fin_guard.get((a - 1, m))
                        if g is not None:
                            V.wait_ge(SM["dX1"].h, g["x1_wr"])
                        x1cs[m] = vsm.inc(V.scalar_tensor_tensor(
                            x1b[m][:, :], PS[:, 2048 + 1024 * m: 3072 + 1024 * m],
                            ST["lrec0" if m == 0 else "lrec1"], xqb[m][:, :],
                            OP.mult, OP.add))
                    prev_fin_v = x1cs[1]
                    for m in range(2):
                        g = fin_guard.get((a - 1, m))
                        x1c = x1cs[m]
                        v_red = vch.inc(V.reduce_sum(ST["ssum"], x1b[m][:, :],
                                                     AX.X))
                        Sc.wait_ge(vsm.h, x1c)
                        Sc.wait_ge(vch.h, v_red)
                        s_sq = ssm.inc(Sc.activation(junk[:, :], x1b[m][:, :],
                                                     AF.Square,
                                                     accum_out=ST["sqs"]))
                        V.wait_ge(vch.h, v_red)
                        c1 = vch.inc(V.tensor_scalar_mul(ST["negm"], ST["ssum"],
                                                         -1.0 / 1024.0))
                        V.wait_ge(vch.h, c1)
                        c2 = vch.inc(V.tensor_tensor(ST["m2"], ST["negm"],
                                                     ST["negm"], OP.mult))
                        V.wait_ge(ssm.h, s_sq)
                        c3 = vch.inc(V.tensor_scalar(ST["vpe"], ST["sqs"],
                                                     1.0 / 1024.0, EPS,
                                                     OP.mult, OP.add))
                        V.wait_ge(vch.h, c3)
                        c4 = vch.inc(V.tensor_tensor(ST["vpe2"], ST["vpe"],
                                                     ST["m2"], OP.subtract))
                        Sc.wait_ge(vch.h, c4)
                        s1 = sch.inc(Sc.activation(ST["lnv"], ST["vpe2"], AF.Ln))
                        Sc.wait_ge(sch.h, s1)
                        s2 = ssm.inc(Sc.activation(ST["rs"], ST["lnv"], AF.Exp,
                                                   scale=-0.5))
                        V.wait_ge(ssm.h, s2)
                        ctl = vch.inc(V.tensor_scalar(junk[:, :], x1b[m][:, :],
                                                      ST["negm"], ST["rs"],
                                                      OP.add, OP.mult))
                        V.wait_ge(vch.h, ctl)
                        xac = vsm.inc(V.tensor_tensor(xqb[m][:, :], x1b[m][:, :],
                                                      junk[:, :], OP.add))
                        Sy.wait_ge(vsm.h, xac)
                        ins = Sy.dma_start(
                            x1_sc[256 * a + 128 * m: 256 * a + 128 * (m + 1), :],
                            xqb[m][:, :])
                        SM["dX1"].inc(ins, 16)
                        x1_wr = SM["dX1"].n
                        V.wait_ge(vch.h, c4)
                        c5 = vch.inc(V.reciprocal(ST["iv"], ST["vpe2"]))
                        V.wait_ge(vch.h, c5)
                        c6 = vch.inc(V.tensor_scalar(ST["vrpe"], ST["iv"], -EPS,
                                                     1.0 + EPS, OP.mult, OP.add))
                        Sc.wait_ge(vch.h, c6)
                        s3 = sch.inc(Sc.activation(ST["lnv2"], ST["vrpe"], AF.Ln))
                        Sc.wait_ge(sch.h, s3)
                        s4 = sch.inc(Sc.activation(ST["r2"], ST["lnv2"], AF.Exp,
                                                   scale=-0.5))
                        Sc.wait_ge(sch.h, s4)
                        s5 = ssm.inc(Sc.mul(ST["sstar"], ST["rs"], ST["r2"]))
                        V.wait_ge(ssm.h, s5)
                        if g is not None:
                            V.wait_ge(SM["dT2"].h, g["t2_wr"])
                        t2c = vsm.inc(V.tensor_scalar(t2tm[m][:, :], x1b[m][:, :],
                                                      ST["negm"], ST["sstar"],
                                                      OP.add, OP.mult))
                        Sy.wait_ge(vsm.h, t2c)
                        ins = Sy.dma_start(
                            t2_sc[256 * a + 128 * m: 256 * a + 128 * (m + 1), :],
                            t2tm[m][:, :])
                        SM["dT2"].inc(ins, 16)
                        fin_guard[(a, m)] = {"x1_wr": x1_wr,
                                             "t2_wr": SM["dT2"].n}
                return fin

            pending_fin = make_fin()
        pending_fin()
        pending_fin = None

        stage2_end = {k: SM[k].n for k in sem_names}

        tA_end = {}
        tB_end = {}
        vA_ps = {}
        vA_hw = {}
        vB = {}
        gas_c = {}

        def wload(L, wq_guard):
            eb = L % 2
            if L >= 2:
                G.wait_ge(tsm.h, wq_guard)
            else:
                for kk in ("tsm", "vsm", "ssm"):
                    G.wait_ge(SM[kk].h, stage2_end[kk])
            e = L % 32
            ins = G.dma_start((w1A if eb == 0 else w1B)[:, :], W1H[e])
            SM["dW"].inc(ins, 16)
            ins = G.dma_start((w2A if eb == 0 else w2B)[:, :], W2H[e])
            SM["dW"].inc(ins, 16)
            return SM["dW"].n

        wq_c = {0: wload(0, None), 1: wload(1, None)}
        tb = [t2T, pw_sb]

        def t2_load_scale(Q, guard_vch=None, guard_tsm=None):
            buf = tb[Q % 2]
            if guard_tsm is not None:
                Sc.wait_ge(tsm.h, guard_tsm)
            if guard_vch is not None:
                Sc.wait_ge(vch.h, guard_vch)
            Sc.wait_ge(SM["dT2"].h, stage2_end["dT2"])
            for dc in range(8):
                ins = Sc.dma_start(
                    buf[:, 1024 * dc: 1024 * (dc + 1)],
                    t2_sc[1024 * Q: 1024 * (Q + 1), 128 * dc: 128 * (dc + 1)],
                    transpose=True)
                SM["dT2L"].inc(ins, 16)
            Sc.wait_ge(SM["dT2L"].h, SM["dT2L"].n)
            Sc.wait_ge(dMISC.h, setup_cnt)
            for dc in range(8):
                ins = Sc.mul(buf[:, 1024 * dc: 1024 * (dc + 1)],
                             buf[:, 1024 * dc: 1024 * (dc + 1)],
                             nsc_sb[:, dc:dc + 1])
            return ssm.inc(ins)

        next_scale_c = t2_load_scale(0, guard_tsm=stage2_end["tsm"])
        prevQ = None
        for Q in range(4):
            L0 = 32 * Q
            act = tb[Q % 2]
            pwq = tb[(Q + 1) % 2]
            t2scale_c = next_scale_c
            if prevQ:
                Sy.wait_ge(SM["dOUT"].h, prevQ["out_wr"])
            Sy.wait_ge(SM["dX1"].h, stage2_end["dX1"])
            Sy.wait_ge(SM["dT2"].h, stage2_end["dT2"])
            Sy.wait_ge(ssm.h, stage2_end["ssm"])
            src_x1 = bass.AP(x1_sc.tensor, 1024 * Q * D,
                             [[D, 128], [128 * D, 8], [1, 1024]])
            ins = Sy.dma_start(oacc[:, :].rearrange("p (s n) -> p s n", s=8),
                               src_x1)
            SM["dOA"].inc(ins, 16)
            oa_c = SM["dOA"].n
            gate_e = {}
            for s in range(8):
                T.wait_ge(ssm.h, t2scale_c)
                if s > 0:
                    T.wait_ge(vsm.h, gate_e[s - 1])
                elif prevQ:
                    T.wait_ge(vsm.h, prevQ["lastA_v"])
                for dc in range(8):
                    ins = T.matmul(PS[:, 0:32],
                                   act[:, 1024 * dc + 128 * s: 1024 * dc + 128 * s + 128],
                                   gwT_sb[:, 32 * dc: 32 * (dc + 1)],
                                   start=(dc == 0), stop=(dc == 7))
                tg = tsm.inc(ins)
                V.wait_ge(tsm.h, tg)
                gate_e[s] = vsm.inc(V.tensor_tensor(g_sb[:, 32 * s:32 * (s + 1)],
                                                    PS[:, 0:32], gbrep[:, :],
                                                    OP.add))
            V.wait_ge(vsm.h, gate_e[7])
            if prevQ:
                V.wait_ge(tsm.h, prevQ["lastA_mm"])
            ap_j = bass.AP(g_sb, 0, [[256, 128], [32, 8], [1, 32], [0, 32]])
            ap_e = bass.AP(g_sb, 0, [[256, 128], [32, 8], [0, 32], [1, 32]])
            ap_pw_w = bass.AP(pwq, 0, [[8192, 128], [1024, 8], [32, 32], [1, 32]])
            ap_pw_rd = bass.AP(pwq, 0, [[8192, 128], [32, 256], [1, 32]])
            c1 = vch.inc(V.tensor_tensor(ap_pw_w, ap_j, ap_e, OP.is_lt))
            V.wait_ge(vch.h, c1)
            c2 = vch.inc(V.tensor_reduce(rnkf[:, :], ap_pw_rd, AX.X, OP.add))
            V.wait_ge(vch.h, c2)
            c3 = vch.inc(V.tensor_tensor(ap_pw_w, ap_j, ap_e, OP.is_equal))
            V.wait_ge(vch.h, c3)
            ap_tri = bass.AP(tri_sb, 0, [[1024, 128], [0, 8], [1, 1024]])
            ap_pw3 = bass.AP(pwq, 0, [[8192, 128], [1024, 8], [1, 1024]])
            c4 = vch.inc(V.tensor_tensor(ap_pw3, ap_pw3, ap_tri, OP.mult))
            V.wait_ge(vch.h, c4)
            c5 = vch.inc(V.tensor_reduce(msk_sb[:, :], ap_pw_rd, AX.X, OP.add))
            V.wait_ge(vch.h, c5)
            c6 = vch.inc(V.tensor_tensor(msk_sb[:, :], msk_sb[:, :], rnkf[:, :],
                                         OP.add))
            V.wait_ge(vch.h, c6)
            c7 = vch.inc(V.tensor_single_scalar(msk_sb[:, :], msk_sb[:, :], 3.5,
                                                OP.is_lt))
            ap_g2 = bass.AP(g_sb, 0, [[256, 128], [32, 8], [1, 32]])
            c8 = vch.inc(V.tensor_reduce(nmx[:, :], ap_g2, AX.X, OP.max,
                                         negate=True))
            Sc.wait_ge(vch.h, c8)
            for s in range(8):
                ins = Sc.activation(me_sb[:, 32 * s:32 * (s + 1)],
                                    g_sb[:, 32 * s:32 * (s + 1)], AF.Exp,
                                    bias=nmx[:, s:s + 1])
            sexp = ssm.inc(ins)
            V.wait_ge(ssm.h, sexp)
            V.wait_ge(vch.h, c7)
            c9 = vch.inc(V.tensor_tensor(me_sb[:, :], me_sb[:, :], msk_sb[:, :],
                                         OP.mult))
            V.wait_ge(vch.h, c9)
            ap_me2 = bass.AP(me_sb, 0, [[256, 128], [32, 8], [1, 32]])
            c10 = vch.inc(V.tensor_reduce(zr[:, :], ap_me2, AX.X, OP.add))
            V.wait_ge(vch.h, c10)
            c11 = vch.inc(V.reciprocal(izr[:, :], zr[:, :]))
            V.wait_ge(vch.h, c11)
            ap_izr = bass.AP(izr, 0, [[8, 128], [1, 8], [0, 32]])
            cmb_c = vsm.inc(V.tensor_tensor(cmb[:, :], me_sb[:, :], ap_izr,
                                            OP.mult))
            if Q < 3:
                next_scale_c = t2_load_scale(
                    Q + 1, guard_vch=c5,
                    guard_tsm=prevQ["lastA_mm"] if prevQ else None)
            def emit_b2_block(cmb_c=cmb_c, Q=Q, oa_c=oa_c, prevQ=prevQ):
                trev = {}
                for s in range(8):
                    T.wait_ge(vsm.h, cmb_c)
                    if prevQ and s == 0:
                        T.wait_ge(vsm.h, prevQ["lastB_v"])
                    if s >= 1:
                        T.wait_ge(vsm.h, trev[s - 1])
                    tt = tsm.inc(T.transpose(
                        PS[0:32, 2048 + 128 * s: 2048 + 128 * (s + 1)],
                        cmb[:, 32 * s: 32 * (s + 1)], idf_sb[:, :]))
                    V.wait_ge(tsm.h, tt)
                    trev[s] = vsm.inc(V.tensor_copy(
                        cmbT[:, 128 * s: 128 * (s + 1)],
                        PS[0:32, 2048 + 128 * s: 2048 + 128 * (s + 1)]))
                for s in range(8):
                    T.wait_ge(vsm.h, trev[7])
                    if s >= 2:
                        T.wait_ge(vsm.h, vB[("b2", Q, s - 2)])
                    for hh in range(2):
                        ins = T.matmul(PS[:, 2048 + 1024 * (s % 2) + 512 * hh:
                                          2048 + 1024 * (s % 2) + 512 * (hh + 1)],
                                       cmbT[:, 128 * s: 128 * (s + 1)],
                                       b2_sb[:, 512 * hh: 512 * (hh + 1)],
                                       start=True, stop=True)
                    bt = tsm.inc(ins)
                    V.wait_ge(tsm.h, bt)
                    if s == 0:
                        V.wait_ge(SM["dOA"].h, oa_c)
                    b2add = V.tensor_tensor(
                        oacc[:, 1024 * s: 1024 * (s + 1)],
                        oacc[:, 1024 * s: 1024 * (s + 1)],
                        PS[:, 2048 + 1024 * (s % 2): 3072 + 1024 * (s % 2)],
                        OP.add)
                    vB[("b2", Q, s)] = vsm.inc(b2add)
            lastA_v = None
            for e in range(32):
                L = L0 + e
                eb = L % 2
                w1 = w1A if eb == 0 else w1B
                w2 = w2A if eb == 0 else w2B
                T.wait_ge(SM["dW"].h, wq_c[L])
                if e == 0:
                    T.wait_ge(vsm.h, gate_e[7])
                def emit_hw(Lu):
                    Lp, up = Lu
                    fcp, thp = up % 8, up // 8
                    V.wait_ge(ssm.h, gas_c[Lu])
                    ins = V.tensor_tensor(
                        hwT[:, 1024 * fcp + 512 * thp:
                            1024 * fcp + 512 * (thp + 1)],
                        gas[up % 2][:, :], lc2[up % 2][:, :], OP.mult)
                    vA_hw[Lu] = vsm.inc(ins)

                for u in range(16):
                    fc, th = u % 8, u // 8
                    ub = u % 2
                    if u >= 2:
                        T.wait_ge(vsm.h, vA_ps[(L, u - 2)])
                    elif L > L0:
                        T.wait_ge(vsm.h, vA_ps[(L - 1, 14 + u)])
                    elif prevQ:
                        T.wait_ge(vsm.h, prevQ["lastA_v"])
                    for d in range(8):
                        lg = w1[:, 1024 * d + 128 * fc: 1024 * d + 128 * fc + 128]
                        ll = w1[:, 8192 + 1024 * d + 128 * fc:
                                8192 + 1024 * d + 128 * fc + 128]
                        rhs = act[:, 1024 * d + 512 * th: 1024 * d + 512 * (th + 1)]
                        T.matmul(bank(2 * ub), lg, rhs, start=(d == 0),
                                 stop=(d == 7))
                        ins = T.matmul(bank(2 * ub + 1), ll, rhs,
                                       start=(d == 0), stop=(d == 7))
                    tA_end[(L, u)] = tsm.inc(ins)
                    V.wait_ge(tsm.h, tA_end[(L, u)])
                    gprev = gas_c.get((L, u - 2) if u > 1 else (L - 1, 14 + u))
                    if gprev is not None:
                        V.wait_ge(ssm.h, gprev)
                    v1 = vsm.inc(V.tensor_scalar(gmin[ub][:, :], bank(2 * ub),
                                                 b1gb[:, 8 * e + fc: 8 * e + fc + 1],
                                                 7.0, OP.add, OP.min))
                    vsm.inc(V.tensor_scalar(lc1[:, :], bank(2 * ub + 1),
                                            b1lb[:, 8 * e + fc: 8 * e + fc + 1],
                                            7.0, OP.add, OP.min))
                    vA_ps[(L, u)] = vsm.n
                    vsm.inc(V.tensor_scalar(lc2[ub][:, :], lc1[:, :], -7.0, 1.0,
                                            OP.max, OP.add))
                    Sc.wait_ge(vsm.h, v1)
                    gas_c[(L, u)] = ssm.inc(
                        Sc.activation(gas[ub][:, :], gmin[ub][:, :],
                                      AF.Gelu_apprx_sigmoid))
                    prev_u = (L, u - 1) if u > 0 else (L - 1, 15)
                    if prev_u in gas_c:
                        emit_hw(prev_u)
                emit_hw((L, 15))
                lastA_mm = tA_end[(L, 15)]
                lastA_v = vA_hw[(L, 15)]
                if e == 0:
                    emit_b2_block()
                for s in range(8):
                    ob = s % 2
                    T.wait_ge(vsm.h, vA_hw[(L, 7 if s < 4 else 15)])
                    if s >= 2:
                        T.wait_ge(vsm.h, vB[(L, s - 2)])
                    elif L > 0 and (L - 1, 6 + s) in vB:
                        T.wait_ge(vsm.h, vB[(L - 1, 6 + s)])
                    if e == 0:
                        T.wait_ge(vsm.h, vB[("b2", Q, s)])
                    for f in range(8):
                        lhsT = hwT[:, 1024 * f + 128 * s: 1024 * f + 128 * s + 128]
                        for hh in range(2):
                            ins = T.matmul(PS[:, 2048 + 1024 * ob + 512 * hh:
                                              2048 + 1024 * ob + 512 * (hh + 1)],
                                           lhsT,
                                           w2[:, 1024 * f + 512 * hh:
                                              1024 * f + 512 * (hh + 1)],
                                           start=(f == 0), stop=(f == 7))
                    tB_end[(L, s)] = tsm.inc(ins)
                    V.wait_ge(tsm.h, tB_end[(L, s)])
                    if e == 0:
                        V.wait_ge(SM["dOA"].h, oa_c)
                    ins = V.scalar_tensor_tensor(
                        oacc[:, 1024 * s: 1024 * (s + 1)],
                        PS[:, 2048 + 1024 * ob: 3072 + 1024 * ob],
                        cmb[:, 32 * s + e: 32 * s + e + 1],
                        oacc[:, 1024 * s: 1024 * (s + 1)],
                        OP.mult, OP.add)
                    vB[(L, s)] = vsm.inc(ins)
                tB_end[L] = tsm.n
                if L + 2 < 128:
                    wq_c[L + 2] = wload(L + 2, tB_end[L])
            for s in range(8):
                Sy.wait_ge(vsm.h, vB[(L0 + 31, s)])
                ins = Sy.dma_start(
                    out[1024 * Q + 128 * s: 1024 * Q + 128 * (s + 1), :],
                    oacc[:, 1024 * s: 1024 * (s + 1)])
                SM["dOUT"].inc(ins, 16)
            prevQ = {"lastA_mm": lastA_mm, "lastA_v": lastA_v,
                     "tB_last": tB_end[L0 + 31], "out_wr": SM["dOUT"].n,
                     "lastB_v": vB[(L0 + 31, 7)]}

        Sy.wait_ge(SM["dOUT"].h, SM["dOUT"].n)

    stack.close()
    return nc


_NC_CACHE = None


def kernel(L0_attn_x, L0_attn_W_qkv, L0_ffn_norm_scale, L0_ffn_gate_w,
           L0_ffn_gate_b, L0_ffn_w1, L0_ffn_b1, L0_ffn_w2, L0_ffn_b2):
    global _NC_CACHE
    x = np.asarray(L0_attn_x, np.float32)
    Wqkv = np.ascontiguousarray(np.asarray(L0_attn_W_qkv, np.float32))
    norm_scale = np.asarray(L0_ffn_norm_scale, np.float32)
    gate_w = np.asarray(L0_ffn_gate_w, np.float32)
    gate_b = np.asarray(L0_ffn_gate_b, np.float32)
    w1 = np.asarray(L0_ffn_w1, np.float32)
    b1 = np.asarray(L0_ffn_b1, np.float32)
    w2 = np.asarray(L0_ffn_w2, np.float32)
    b2 = np.asarray(L0_ffn_b2, np.float32)

    wkv_h = np.ascontiguousarray(Wqkv[:, 1024:3072])
    wq_h = np.ascontiguousarray(Wqkv[:, 0:1024])
    gwT_h = np.ascontiguousarray(gate_w.T)
    gbr_h = np.ascontiguousarray(np.broadcast_to(gate_b.reshape(1, 32), (128, 32)))
    nsc_h = np.ascontiguousarray(norm_scale.reshape(8, 128).T)
    b1g_h = np.ascontiguousarray(
        b1[:, 0::2].reshape(32, 8, 128).transpose(2, 0, 1).reshape(128, 256))
    b1l_h = np.ascontiguousarray(
        b1[:, 1::2].reshape(32, 8, 128).transpose(2, 0, 1).reshape(128, 256))
    w1g = w1[:, 0::2, :].transpose(0, 2, 1)
    w1l = w1[:, 1::2, :].transpose(0, 2, 1)
    def w1_re(a):
        return (a.reshape(32, 8, 128, 8, 128).transpose(0, 2, 1, 3, 4)
                .reshape(32, 128, 8192))
    W1H_h = np.ascontiguousarray(
        np.concatenate([w1_re(w1g), w1_re(w1l)], axis=2))
    W2H_h = np.ascontiguousarray(
        w2.transpose(0, 2, 1).reshape(32, 8, 128, 1024)
        .transpose(0, 2, 1, 3).reshape(32, 128, 8192))
    b2m_h = np.ascontiguousarray(b2)
    onesr_h = np.ones((1, 128), ml_dtypes.bfloat16)
    onescol_h = np.ones((128, 1), ml_dtypes.bfloat16)
    trim_h = np.ascontiguousarray(np.broadcast_to(
        (np.arange(32)[None, :] < np.arange(32)[:, None]).astype(
            ml_dtypes.bfloat16).reshape(1, 1024), (128, 1024)))

    in_maps = []
    for c in range(8):
        seq, sub = c // 4, c % 4
        xs = x[seq]
        xq_h = np.ascontiguousarray(
            xs.reshape(128, 128, 1024)[sub::4].reshape(4096, 1024))
        m = np.zeros((8, 128, 128), np.float32)
        for j in range(8):
            d = j - sub
            if d == 0:
                p = np.arange(128)
                m[j] = np.where(p[:, None] > p[None, :], NEG, 0.0)
            elif d > 0:
                m[j] = NEG
        in_maps.append({
            "xk": np.ascontiguousarray(xs), "xq": xq_h,
            "wkv": wkv_h, "wq": wq_h, "gwT": gwT_h, "gbr": gbr_h,
            "nsc": nsc_h, "b1g": b1g_h, "b1l": b1l_h, "b2m": b2m_h,
            "W1H": W1H_h, "W2H": W2H_h, "masks": m, "trim": trim_h,
            "onesr": onesr_h, "onescol": onescol_h,
            "identf": np.eye(128, dtype=np.float32),
        })

    if _NC_CACHE is None:
        _NC_CACHE = build_program()
    res = run_bass_kernel_spmd(_NC_CACHE, in_maps, core_ids=list(range(8)))

    out_full = np.empty((2, 16384, 1024), np.float32)
    for c in range(8):
        seq, sub = c // 4, c % 4
        view = out_full[seq].reshape(128, 128, 1024)
        view[sub::4] = res.results[c]["out"].reshape(32, 128, 1024)
    return out_full


# version 23
# speedup vs baseline: 1.1408x; 1.1408x over previous
import contextlib
import numpy as np
import ml_dtypes
import concourse.bass as bass
import concourse.mybir as mybir
from concourse.bass_utils import run_bass_kernel_spmd

F32 = mybir.dt.float32
BF16 = mybir.dt.bfloat16
AF = mybir.ActivationFunctionType
OP = mybir.AluOpType
AX = mybir.AxisListType

S = 16384
D = 1024
TC = 4096
NEG = -3.0e38
EPS = 1e-5


class Ctr:
    __slots__ = ("h", "n")
    def __init__(self, h):
        self.h = h
        self.n = 0
    def inc(self, ins, k=1):
        ins.then_inc(self.h, k)
        self.n += k
        return self.n


def build_program(debug=False):
    nc = bass.Bass("TRN2", target_bir_lowering=False)
    scratch_kind = "ExternalOutput" if debug else "Internal"

    xk = nc.dram_tensor("xk", [S, D], F32, kind="ExternalInput").ap()
    xq = nc.dram_tensor("xq", [TC, D], F32, kind="ExternalInput").ap()
    wkv = nc.dram_tensor("wkv", [D, 2048], F32, kind="ExternalInput").ap()
    wq = nc.dram_tensor("wq", [D, D], F32, kind="ExternalInput").ap()
    gwT = nc.dram_tensor("gwT", [D, 32], F32, kind="ExternalInput").ap()
    gbr = nc.dram_tensor("gbr", [128, 32], F32, kind="ExternalInput").ap()
    nsc = nc.dram_tensor("nsc", [128, 8], F32, kind="ExternalInput").ap()
    b1g = nc.dram_tensor("b1g", [128, 256], F32, kind="ExternalInput").ap()
    b1l = nc.dram_tensor("b1l", [128, 256], F32, kind="ExternalInput").ap()
    b2m = nc.dram_tensor("b2m", [32, 1024], F32, kind="ExternalInput").ap()
    W1H = nc.dram_tensor("W1H", [32, 128, 16384], F32, kind="ExternalInput").ap()
    W2H = nc.dram_tensor("W2H", [32, 128, 8192], F32, kind="ExternalInput").ap()
    masks = nc.dram_tensor("masks", [8, 128, 128], F32, kind="ExternalInput").ap()
    trim = nc.dram_tensor("trim", [128, 1024], BF16, kind="ExternalInput").ap()
    onesr = nc.dram_tensor("onesr", [1, 128], BF16, kind="ExternalInput").ap()
    identf = nc.dram_tensor("identf", [128, 128], F32, kind="ExternalInput").ap()
    onescol = nc.dram_tensor("onescol", [128, 1], BF16, kind="ExternalInput").ap()
    out = nc.dram_tensor("out", [TC, D], F32, kind="ExternalOutput").ap()

    xln_sc = nc.dram_tensor("xln_sc", [S, D], BF16, kind=scratch_kind).ap()
    xlnq_sc = nc.dram_tensor("xlnq_sc", [TC, D], BF16, kind=scratch_kind).ap()
    kT_sc = nc.dram_tensor("kT_sc", [16, 128, 8192], BF16, kind=scratch_kind).ap()
    v_sc = nc.dram_tensor("v_sc", [16, 128, 8192], BF16, kind=scratch_kind).ap()
    qT_sc = nc.dram_tensor("qT_sc", [D, TC], BF16, kind=scratch_kind).ap()
    x1_sc = nc.dram_tensor("x1_sc", [TC, D], F32, kind=scratch_kind).ap()
    t2_sc = nc.dram_tensor("t2_sc", [TC, D], BF16, kind=scratch_kind).ap()

    w1A = nc.alloc_sbuf_tensor("w1A", [128, 16384], BF16)
    w1B = nc.alloc_sbuf_tensor("w1B", [128, 16384], BF16)
    w2A = nc.alloc_sbuf_tensor("w2A", [128, 8192], BF16)
    w2B = nc.alloc_sbuf_tensor("w2B", [128, 8192], BF16)
    t2T = nc.alloc_sbuf_tensor("t2T", [128, 8192], BF16)
    hwT = nc.alloc_sbuf_tensor("hwT", [128, 8192], BF16)
    oacc = nc.alloc_sbuf_tensor("oacc", [128, 8192], F32)

    masks_sb = nc.alloc_sbuf_tensor("masks_sb", [128, 1024], F32)
    tri_sb = nc.alloc_sbuf_tensor("tri_sb", [128, 1024], BF16)
    gas = [nc.alloc_sbuf_tensor(f"gas{i}", [128, 512], BF16) for i in range(2)]
    gmin = [nc.alloc_sbuf_tensor(f"gmin{i}", [128, 512], F32) for i in range(2)]
    lc1 = nc.alloc_sbuf_tensor("lc1", [128, 512], F32)
    lc2 = [nc.alloc_sbuf_tensor(f"lc2{i}", [128, 512], BF16) for i in range(2)]
    b1gb = nc.alloc_sbuf_tensor("b1gb", [128, 256], F32)
    b1lb = nc.alloc_sbuf_tensor("b1lb", [128, 256], F32)
    gwT_sb = nc.alloc_sbuf_tensor("gwT_sb", [128, 256], BF16)
    gbrep = nc.alloc_sbuf_tensor("gbrep", [128, 32], F32)
    nsc_sb = nc.alloc_sbuf_tensor("nsc_sb", [128, 8], F32)
    idf_sb = nc.alloc_sbuf_tensor("idf_sb", [128, 128], F32)
    b2_sb = nc.alloc_sbuf_tensor("b2_sb", [32, 1024], BF16)
    cmbT = nc.alloc_sbuf_tensor("cmbT", [32, 1024], BF16)
    onecb = nc.alloc_sbuf_tensor("onecb", [128, 1], BF16)
    g_sb = nc.alloc_sbuf_tensor("g_sb", [128, 256], F32)
    me_sb = nc.alloc_sbuf_tensor("me_sb", [128, 256], F32)
    msk_sb = nc.alloc_sbuf_tensor("msk_sb", [128, 256], F32)
    rnkf = nc.alloc_sbuf_tensor("rnkf", [128, 256], F32)
    cmb = nc.alloc_sbuf_tensor("cmb", [128, 256], F32)
    nmx = nc.alloc_sbuf_tensor("nmx", [128, 8], F32)
    zr = nc.alloc_sbuf_tensor("zr", [128, 8], F32)
    izr = nc.alloc_sbuf_tensor("izr", [128, 8], F32)
    stt = nc.alloc_sbuf_tensor("stt", [128, 32], F32)
    pw_sb = nc.alloc_sbuf_tensor("pw_sb", [128, 8192], BF16)

    ST = {n: stt[:, i:i + 1] for i, n in enumerate(
        ["ssum", "sqs", "negm", "m2", "vpe", "vpe2", "lnv", "rs",
         "iv", "vrpe", "lnv2", "r2", "sstar", "lrec0", "lrec1"])}

    Wkv_sb = w1A
    Wq_sb = w2B
    xlnT = [w2A[:, 0:4096], w2A[:, 4096:8192]]
    kTsb = [t2T[:, 2560 + 512 * i: 3072 + 512 * i] for i in range(2)]
    vsb = [t2T[:, 3584 + 1024 * i: 4608 + 1024 * i] for i in range(2)]
    xln_tm = [hwT[:, 1024 * i: 1024 * (i + 1)] for i in range(2)]
    xtm = [oacc[:, 1024 * i: 1024 * (i + 1)] for i in range(2)]
    junk = oacc[:, 5120:6144]
    kTst = [w1A[:, 0:8192], w1A[:, 8192:16384]]
    vst = [w1B[:, 0:8192], w1B[:, 8192:16384]]
    qTsb = [hwT[:, 2048:4096], hwT[:, 4096:6144]]
    PT = [t2T[:, 0:256], t2T[:, 256:512], t2T[:, 2560:2816]]
    t2tm = [t2T[:, 512 + 1024 * i: 1536 + 1024 * i] for i in range(2)]
    xqb = [oacc[:, 1024 + 1024 * i: 2048 + 1024 * i] for i in range(2)]
    x1b = [oacc[:, 3072 + 1024 * i: 4096 + 1024 * i] for i in range(2)]
    pw = pw_sb

    PS = nc.alloc_psum_tensor("PS", [128, 4096], F32)
    def bank(i, w=512):
        return PS[:, 512 * i: 512 * i + w]

    stack = contextlib.ExitStack()
    sem_names = ["dMISC", "dXK", "dXLN", "dXLT", "dKT", "dVV", "dQT", "dKTS",
                 "dVS", "dQS", "dXQ", "dX1", "dT2", "dT2L", "dW", "dOA",
                 "dOUT", "tsm", "vsm", "ssm", "vch", "sch"]
    SM = {n: Ctr(stack.enter_context(nc.semaphore(n))) for n in sem_names}
    block = stack.enter_context(nc.Block())

    @block.gpsimd
    def _(gp):
        T = nc.tensor; V = nc.vector; Sc = nc.scalar; Sy = nc.sync; G = nc.gpsimd
        tsm, vsm, ssm, vch, sch = (SM[k] for k in ["tsm", "vsm", "ssm", "vch", "sch"])
        dMISC, dXK, dXLN, dXLT = (SM[k] for k in ["dMISC", "dXK", "dXLN", "dXLT"])
        dKT, dVV, dQT, dKTS, dVS, dQS = (SM[k] for k in
                                         ["dKT", "dVV", "dQT", "dKTS", "dVS", "dQS"])
        dXQ, dX1, dT2, dT2L, dW, dOA, dOUT = (SM[k] for k in
            ["dXQ", "dX1", "dT2", "dT2L", "dW", "dOA", "dOUT"])

        def fm_src(dram_ap, n):
            return bass.AP(dram_ap.tensor, 0, [[n, 128], [128 * n, 8], [1, n]])
        dMISC.inc(G.dma_start(Wkv_sb[:, :].rearrange("p (c n) -> p c n", c=8),
                              fm_src(wkv, 2048)), 16)
        dMISC.inc(G.dma_start(Wq_sb[:, 0:8192].rearrange("p (c n) -> p c n", c=8),
                              fm_src(wq, 1024)), 16)
        dMISC.inc(G.dma_start(gwT_sb[:, :].rearrange("p (c n) -> p c n", c=8),
                              fm_src(gwT, 32)), 16)
        dMISC.inc(G.dma_start(gbrep[:, :], gbr), 16)
        dMISC.inc(G.dma_start(nsc_sb[:, :], nsc), 16)
        dMISC.inc(G.dma_start(b1gb[:, :], b1g), 16)
        dMISC.inc(G.dma_start(b1lb[:, :], b1l), 16)
        dMISC.inc(G.dma_start(onecb[:, :], onescol), 16)
        dMISC.inc(G.dma_start(
            masks_sb[:, :].rearrange("p (j r) -> p j r", j=8),
            bass.AP(masks.tensor, 0, [[128, 128], [128 * 128, 8], [1, 128]])), 16)
        dMISC.inc(G.dma_start(tri_sb[:, :], trim), 16)
        dMISC.inc(G.dma_start(idf_sb[:, :], identf), 16)
        dMISC.inc(G.dma_start(b2_sb[:, :], b2m), 16)
        setup_cnt = dMISC.n

        xtm8 = [oacc[:, 1024 * i: 1024 * (i + 1)] for i in range(8)]
        xln8 = [hwT[:, 1024 * i: 1024 * (i + 1)] for i in range(8)]
        stt4 = nc.alloc_sbuf_tensor("stt4x", [128, 64], F32)
        def st4(par, name):
            i = ["ssum", "sqs", "negm", "m2", "vpe", "vpe2", "lnv", "rs"].index(name)
            return stt4[:, 32 * par + 4 * i: 32 * par + 4 * (i + 1)]

        ln_prev = {"vx": [None] * 8, "sq": [None] * 8, "wr": [None] * 8,
                   "c4": [0, 0]}
        mm_end = {}
        ln_tasks = [("kv", xk, sg, SM["dXK"], xln_sc) for sg in range(32)] + \
                   [("q", xq, sg, SM["dXQ"], xlnq_sc) for sg in range(8)]

        def ln_super(task_i):
            _, src_dram, sg, xd_ctr, sc_dram = ln_tasks[task_i]
            par = task_i % 2
            for tau in range(4):
                slot = 4 * par + tau
                if ln_prev["vx"][slot] is not None:
                    Sy.wait_ge(vsm.h, ln_prev["vx"][slot])
                    Sy.wait_ge(ssm.h, ln_prev["sq"][slot])
                t = 4 * sg + tau
                ins = Sy.dma_start(xtm8[slot][:, :],
                                   src_dram[128 * t:128 * (t + 1), :])
                xd_ctr.inc(ins, 16)
            ld_c = xd_ctr.n
            V.wait_ge(xd_ctr.h, ld_c)
            V.wait_ge(vch.h, ln_prev["c4"][par])
            r4 = vch.inc(V.tensor_reduce(
                st4(par, "ssum"),
                bass.AP(oacc, 4096 * par, [[8192, 128], [1024, 4], [1, 1024]]),
                AX.X, OP.add))
            sq_c = [None] * 4
            for tau in range(4):
                slot = 4 * par + tau
                Sc.wait_ge(xd_ctr.h, ld_c)
                Sc.wait_ge(vch.h, ln_prev["c4"][par])
                if ln_prev["vx"][slot] is not None:
                    Sc.wait_ge(vsm.h, ln_prev["vx"][slot])
                if ln_prev["wr"][slot] is not None:
                    Sc.wait_ge(SM["dXLN"].h, ln_prev["wr"][slot])
                sq_c[tau] = ssm.inc(Sc.activation(
                    xln8[slot][:, :], xtm8[slot][:, :], AF.Square,
                    accum_out=st4(par, "sqs")[:, tau:tau + 1]))
            V.wait_ge(vch.h, r4)
            c1 = vch.inc(V.tensor_scalar_mul(st4(par, "negm"), st4(par, "ssum"),
                                             -1.0 / 1024.0))
            V.wait_ge(vch.h, c1)
            c2 = vch.inc(V.tensor_tensor(st4(par, "m2"), st4(par, "negm"),
                                         st4(par, "negm"), OP.mult))
            V.wait_ge(ssm.h, sq_c[3])
            c3 = vch.inc(V.tensor_scalar(st4(par, "vpe"), st4(par, "sqs"),
                                         1.0 / 1024.0, EPS, OP.mult, OP.add))
            V.wait_ge(vch.h, c3)
            c4 = vch.inc(V.tensor_tensor(st4(par, "vpe2"), st4(par, "vpe"),
                                         st4(par, "m2"), OP.subtract))
            Sc.wait_ge(vch.h, c4)
            s1 = sch.inc(Sc.activation(st4(par, "lnv"), st4(par, "vpe2"), AF.Ln))
            Sc.wait_ge(sch.h, s1)
            s2 = ssm.inc(Sc.activation(st4(par, "rs"), st4(par, "lnv"), AF.Exp,
                                       scale=-0.5))
            for tau in range(4):
                slot = 4 * par + tau
                t = 4 * sg + tau
                V.wait_ge(ssm.h, s2)
                if ln_prev["wr"][slot] is not None:
                    V.wait_ge(SM["dXLN"].h, ln_prev["wr"][slot])
                vx = vsm.inc(V.tensor_scalar(
                    xln8[slot][:, :], xtm8[slot][:, :],
                    st4(par, "negm")[:, tau:tau + 1],
                    st4(par, "rs")[:, tau:tau + 1],
                    OP.add, OP.mult))
                ln_prev["vx"][slot] = vx
                ln_prev["sq"][slot] = sq_c[tau]
                Sy.wait_ge(vsm.h, vx)
                ins = Sy.dma_start(sc_dram[128 * t:128 * (t + 1), :],
                                   xln8[slot][:, :])
                SM["dXLN"].inc(ins, 16)
                ln_prev["wr"][slot] = SM["dXLN"].n
            ln_prev["c4"][par] = c4
            return SM["dXLN"].n

        ln_wr = {}
        xlt_c = {}

        def ln_transposes(task_i):
            _, src_dram, sg, xd_ctr, sc_dram = ln_tasks[task_i]
            Sy.wait_ge(SM["dXLN"].h, ln_wr[task_i])
            if task_i >= 2:
                Sy.wait_ge(tsm.h, mm_end[task_i - 2])
            for dc in range(8):
                ins = Sy.dma_start(
                    xlnT[task_i % 2][:, 512 * dc: 512 * (dc + 1)],
                    sc_dram[512 * sg: 512 * (sg + 1),
                            128 * dc: 128 * (dc + 1)],
                    transpose=True)
                SM["dXLT"].inc(ins, 16)
            xlt_c[task_i] = SM["dXLT"].n

        kT_wr = {0: None, 1: None}
        v_wr = {0: None, 1: None}
        vevac_roll = {0: None, 1: None}
        kevac_roll = {0: None, 1: None}

        def kv_mms(task_i, sg):
            T.wait_ge(SM["dXLT"].h, xlt_c[task_i])
            if task_i == 0:
                T.wait_ge(dMISC.h, setup_cnt)
            for dout in range(8):
                pb = dout % 2
                if kevac_roll[pb] is not None:
                    T.wait_ge(vsm.h, kevac_roll[pb])
                for dcin in range(8):
                    lhs = Wkv_sb[:, 2048 * dcin + 128 * dout:
                                 2048 * dcin + 128 * dout + 128]
                    rhs = xlnT[task_i % 2][:, 512 * dcin: 512 * (dcin + 1)]
                    ins = T.matmul(bank(pb), lhs, rhs,
                                   start=(dcin == 0), stop=(dcin == 7))
                tmm = tsm.inc(ins)
                V.wait_ge(tsm.h, tmm)
                if kT_wr[pb] is not None:
                    V.wait_ge(kT_wr[pb][0].h, kT_wr[pb][1])
                kevac_roll[pb] = vsm.inc(V.tensor_copy(kTsb[pb][:, :], bank(pb)))
                Sc.wait_ge(vsm.h, kevac_roll[pb])
                ins = Sc.dma_start(
                    bass.AP(kT_sc.tensor,
                            (sg // 2) * 128 * 8192 + 1024 * dout + 512 * (sg % 2),
                            [[8192, 128], [1, 512]]),
                    kTsb[pb][:, :])
                dKT.inc(ins, 16)
                kT_wr[pb] = (dKT, dKT.n)
            for tau in range(4):
                pb = tau % 2
                if vevac_roll[pb] is not None:
                    T.wait_ge(vsm.h, vevac_roll[pb])
                for dcin in range(8):
                    lhs = xlnT[task_i % 2][:, 512 * dcin + 128 * tau:
                                           512 * dcin + 128 * tau + 128]
                    for hh in range(2):
                        rhs = Wkv_sb[:, 2048 * dcin + 1024 + 512 * hh:
                                     2048 * dcin + 1024 + 512 * (hh + 1)]
                        ins = T.matmul(PS[:, 2048 + 1024 * pb + 512 * hh:
                                          2048 + 1024 * pb + 512 * (hh + 1)],
                                       lhs, rhs, start=(dcin == 0),
                                       stop=(dcin == 7))
                tmm = tsm.inc(ins)
                V.wait_ge(tsm.h, tmm)
                if v_wr[pb] is not None:
                    V.wait_ge(dVV.h, v_wr[pb])
                vevac_roll[pb] = vsm.inc(
                    V.tensor_copy(vsb[pb][:, :],
                                  PS[:, 2048 + 1024 * pb: 3072 + 1024 * pb]))
                t = 4 * sg + tau
                Sc.wait_ge(vsm.h, vevac_roll[pb])
                ins = Sc.dma_start(
                    bass.AP(v_sc.tensor, (t // 8) * 128 * 8192 + 1024 * (t % 8),
                            [[8192, 128], [1, 1024]]),
                    vsb[pb][:, :])
                dVV.inc(ins, 16)
                v_wr[pb] = dVV.n
            mm_end[task_i] = tsm.n

        def q_mms(task_i, sg):
            T.wait_ge(SM["dXLT"].h, xlt_c[task_i])
            for dout in range(8):
                pb = dout % 2
                if kevac_roll[pb] is not None:
                    T.wait_ge(vsm.h, kevac_roll[pb])
                for dcin in range(8):
                    lhs = Wq_sb[:, 1024 * dcin + 128 * dout:
                                1024 * dcin + 128 * dout + 128]
                    rhs = xlnT[task_i % 2][:, 512 * dcin: 512 * (dcin + 1)]
                    ins = T.matmul(bank(pb), lhs, rhs,
                                   start=(dcin == 0), stop=(dcin == 7))
                tmm = tsm.inc(ins)
                V.wait_ge(tsm.h, tmm)
                if kT_wr[pb] is not None:
                    V.wait_ge(kT_wr[pb][0].h, kT_wr[pb][1])
                kevac_roll[pb] = vsm.inc(
                    V.tensor_scalar_mul(kTsb[pb][:, :], bank(pb), 1.0 / 32.0))
                Sc.wait_ge(vsm.h, kevac_roll[pb])
                ins = Sc.dma_start(
                    qT_sc[128 * dout:128 * (dout + 1), 512 * sg: 512 * (sg + 1)],
                    kTsb[pb][:, :])
                dQT.inc(ins, 16)
                kT_wr[pb] = (dQT, dQT.n)
            mm_end[task_i] = tsm.n

        ln_wr[0] = ln_super(0)
        ln_wr[1] = ln_super(1)
        ln_transposes(0)
        for i in range(40):
            if i + 1 < 40:
                ln_transposes(i + 1)
            if ln_tasks[i][0] == "kv":
                kv_mms(i, ln_tasks[i][2])
            else:
                q_mms(i, ln_tasks[i][2])
            if i + 2 < 40:
                ln_wr[i + 2] = ln_super(i + 2)

        stage1_end = {k: SM[k].n for k in sem_names}

        for eng in (V, Sc, Sy, G):
            for kk in ("dXLN", "dKT", "dQT", "dVV", "dXK", "dXQ", "dXLT"):
                eng.wait_ge(SM[kk].h, stage1_end[kk])
            eng.wait_ge(vsm.h, stage1_end["vsm"])
            eng.wait_ge(ssm.h, stage1_end["ssm"])

        def span_load(sigma, buf, guard_t):
            Sy.wait_ge(tsm.h, guard_t)
            Sc.wait_ge(tsm.h, guard_t)
            src_k = bass.AP(kT_sc.tensor, sigma * 128 * 8192,
                            [[8192, 128], [1, 8192]])
            ins = Sy.dma_start(kTst[buf][:, :], src_k)
            SM["dKTS"].inc(ins, 16)
            src_v = bass.AP(v_sc.tensor, sigma * 128 * 8192,
                            [[8192, 128], [1, 8192]])
            ins = Sc.dma_start(vst[buf][:, :], src_v)
            SM["dVS"].inc(ins, 16)
            return SM["dKTS"].n, SM["dVS"].n

        fin_guard = {}
        qT_guard = {}
        spanbuf_lastuse = {}
        prev_super_lastAV = stage1_end["tsm"]
        prev_fin_v = None
        prev_fin_c0 = None
        pending_fin = None
        for a in range(16):
            ntrip = 8 * a + 8
            nspan = a + 1
            G.wait_ge(tsm.h, qT_guard[a - 2] if a >= 2 else stage1_end["tsm"])
            if a == 0:
                G.wait_ge(SM["dQT"].h, stage1_end["dQT"])
            src_q = bass.AP(qT_sc.tensor, 256 * a,
                            [[TC, 128], [128 * TC, 8], [1, 256]])
            ins = G.dma_start(qTsb[a % 2][:, :].rearrange("p (c n) -> p c n", c=8),
                              src_q)
            SM["dQS"].inc(ins, 16)
            qs_c = SM["dQS"].n
            span_c = {}
            for sigma in range(min(2, nspan)):
                span_c[sigma] = span_load(sigma, sigma % 2,
                                          spanbuf_lastuse.get(sigma % 2,
                                                              prev_super_lastAV))
            sc_c = {}
            exp_c = {}
            avt_c = {}
            mask_c = {}
            SB = (0, 1, 3)

            def emit_scores(kappa, a=a, ntrip=ntrip, span_c=span_c, sc_c=sc_c,
                            exp_c=exp_c, avt_c=avt_c, qs_c=qs_c,
                            psl=prev_super_lastAV):
                sigma = kappa // 8
                sub8 = kappa % 8
                sbuf = sigma % 2
                cb = kappa % 3
                T.wait_ge(SM["dKTS"].h, span_c[sigma][0])
                if kappa == 0:
                    T.wait_ge(SM["dQS"].h, qs_c)
                    if a == 0:
                        T.wait_ge(dMISC.h, setup_cnt)
                if kappa >= 3:
                    T.wait_ge(ssm.h, exp_c[kappa - 3])
                for dc in range(8):
                    lhs = kTst[sbuf][:, 1024 * dc + 128 * sub8:
                                     1024 * dc + 128 * sub8 + 128]
                    rhs = qTsb[a % 2][:, 256 * dc: 256 * (dc + 1)]
                    ins = T.matmul(bank(SB[cb], 256), lhs, rhs,
                                   start=(dc == 0), stop=(dc == 7))
                sc_c[kappa] = tsm.inc(ins)
                mask_cnt = None
                j0 = kappa - 8 * a
                if j0 >= 0:
                    for m in range(2):
                        j = j0 - 4 * m
                        if j >= 0:
                            V.wait_ge(tsm.h, sc_c[kappa])
                            ins = V.tensor_tensor(
                                bank(SB[cb], 256)[:, 128 * m:128 * (m + 1)],
                                bank(SB[cb], 256)[:, 128 * m:128 * (m + 1)],
                                masks_sb[:, 128 * j:128 * (j + 1)], OP.add)
                            mask_cnt = vsm.inc(ins)
                mask_c[kappa] = mask_cnt

            def emit_exp(kappa, ntrip=ntrip, sc_c=sc_c, exp_c=exp_c,
                         avt_c=avt_c, mask_c=mask_c, psl=prev_super_lastAV):
                cb = kappa % 3
                if mask_c.get(kappa) is not None:
                    Sc.wait_ge(vsm.h, mask_c[kappa])
                else:
                    Sc.wait_ge(tsm.h, sc_c[kappa])
                Sc.wait_ge(tsm.h, avt_c[kappa - 3] if kappa >= 3 else psl)
                ins = Sc.activation(PT[cb][:, :], bank(SB[cb], 256), AF.Exp)
                exp_c[kappa] = ssm.inc(ins)

            def emit_av(kappa, a=a, ntrip=ntrip, nspan=nspan, span_c=span_c,
                        exp_c=exp_c, avt_c=avt_c):
                sigma = kappa // 8
                sub8 = kappa % 8
                sbuf = sigma % 2
                cb = kappa % 3
                T.wait_ge(ssm.h, exp_c[kappa])
                T.wait_ge(SM["dVS"].h, span_c[sigma][1])
                if kappa == 0 and prev_fin_v is not None:
                    T.wait_ge(vsm.h, prev_fin_v)
                    T.wait_ge(vch.h, prev_fin_c0)
                for m in range(2):
                    for hh in range(2):
                        T.matmul(
                            PS[:, 2048 + 1024 * m + 512 * hh:
                               2048 + 1024 * m + 512 * (hh + 1)],
                            PT[cb][:, 128 * m:128 * (m + 1)],
                            vst[sbuf][:, 1024 * sub8 + 512 * hh:
                                      1024 * sub8 + 512 * (hh + 1)],
                            start=(kappa == 0), stop=(kappa == ntrip - 1))
                    ins = T.matmul(PS[:, 1024 + m:1025 + m],
                                   PT[cb][:, 128 * m:128 * (m + 1)],
                                   onecb[:, :],
                                   start=(kappa == 0 and m == 0),
                                   stop=(kappa == ntrip - 1 and m == 1))
                avt_c[kappa] = tsm.inc(ins)
                if sub8 == 7:
                    spanbuf_lastuse[sigma % 2] = avt_c[kappa]
                    if sigma + 2 < nspan:
                        span_c[sigma + 2] = span_load(sigma + 2, sigma % 2,
                                                      avt_c[kappa])

            emit_scores(0)
            emit_scores(1)
            emit_exp(0)
            if pending_fin is not None:
                pending_fin()
                pending_fin = None
            for kappa in range(ntrip):
                if kappa + 2 < ntrip:
                    emit_scores(kappa + 2)
                if kappa + 1 < ntrip:
                    emit_exp(kappa + 1)
                emit_av(kappa)
            for m in range(2):
                g = fin_guard.get((a - 1, m))
                if g is not None:
                    Sy.wait_ge(SM["dX1"].h, g["x1_wr"])
                ins = Sy.dma_start(xqb[m][:, :],
                                   xq[256 * a + 128 * m: 256 * a + 128 * (m + 1), :])
                SM["dXQ"].inc(ins, 16)
            xq_c = SM["dXQ"].n
            prev_super_lastAV = avt_c[ntrip - 1]
            qT_guard[a] = sc_c[ntrip - 1]

            def make_fin(a=a, lastAV=avt_c[ntrip - 1], xq_c=xq_c):
                def fin():
                    nonlocal prev_fin_v, prev_fin_c0
                    V.wait_ge(tsm.h, lastAV)
                    c0 = vch.inc(V.reciprocal(stt[:, 13:15], PS[:, 1024:1026]))
                    prev_fin_c0 = c0
                    x1cs = {}
                    V.wait_ge(SM["dXQ"].h, xq_c)
                    V.wait_ge(vch.h, c0)
                    for m in range(2):
                        g = fin_guard.get((a - 1, m))
                        if g is not None:
                            V.wait_ge(SM["dX1"].h, g["x1_wr"])
                        x1cs[m] = vsm.inc(V.scalar_tensor_tensor(
                            x1b[m][:, :], PS[:, 2048 + 1024 * m: 3072 + 1024 * m],
                            ST["lrec0" if m == 0 else "lrec1"], xqb[m][:, :],
                            OP.mult, OP.add))
                    prev_fin_v = x1cs[1]
                    for m in range(2):
                        g = fin_guard.get((a - 1, m))
                        x1c = x1cs[m]
                        v_red = vch.inc(V.reduce_sum(ST["ssum"], x1b[m][:, :],
                                                     AX.X))
                        Sc.wait_ge(vsm.h, x1c)
                        Sc.wait_ge(vch.h, v_red)
                        s_sq = ssm.inc(Sc.activation(junk[:, :], x1b[m][:, :],
                                                     AF.Square,
                                                     accum_out=ST["sqs"]))
                        V.wait_ge(vch.h, v_red)
                        c1 = vch.inc(V.tensor_scalar_mul(ST["negm"], ST["ssum"],
                                                         -1.0 / 1024.0))
                        V.wait_ge(vch.h, c1)
                        c2 = vch.inc(V.tensor_tensor(ST["m2"], ST["negm"],
                                                     ST["negm"], OP.mult))
                        V.wait_ge(ssm.h, s_sq)
                        c3 = vch.inc(V.tensor_scalar(ST["vpe"], ST["sqs"],
                                                     1.0 / 1024.0, EPS,
                                                     OP.mult, OP.add))
                        V.wait_ge(vch.h, c3)
                        c4 = vch.inc(V.tensor_tensor(ST["vpe2"], ST["vpe"],
                                                     ST["m2"], OP.subtract))
                        Sc.wait_ge(vch.h, c4)
                        s1 = sch.inc(Sc.activation(ST["lnv"], ST["vpe2"], AF.Ln))
                        Sc.wait_ge(sch.h, s1)
                        s2 = ssm.inc(Sc.activation(ST["rs"], ST["lnv"], AF.Exp,
                                                   scale=-0.5))
                        V.wait_ge(ssm.h, s2)
                        ctl = vch.inc(V.tensor_scalar(junk[:, :], x1b[m][:, :],
                                                      ST["negm"], ST["rs"],
                                                      OP.add, OP.mult))
                        V.wait_ge(vch.h, ctl)
                        xac = vsm.inc(V.tensor_tensor(xqb[m][:, :], x1b[m][:, :],
                                                      junk[:, :], OP.add))
                        Sy.wait_ge(vsm.h, xac)
                        ins = Sy.dma_start(
                            x1_sc[256 * a + 128 * m: 256 * a + 128 * (m + 1), :],
                            xqb[m][:, :])
                        SM["dX1"].inc(ins, 16)
                        x1_wr = SM["dX1"].n
                        V.wait_ge(vch.h, c4)
                        c5 = vch.inc(V.reciprocal(ST["iv"], ST["vpe2"]))
                        V.wait_ge(vch.h, c5)
                        c6 = vch.inc(V.tensor_scalar(ST["vrpe"], ST["iv"], -EPS,
                                                     1.0 + EPS, OP.mult, OP.add))
                        Sc.wait_ge(vch.h, c6)
                        s3 = sch.inc(Sc.activation(ST["lnv2"], ST["vrpe"], AF.Ln))
                        Sc.wait_ge(sch.h, s3)
                        s4 = sch.inc(Sc.activation(ST["r2"], ST["lnv2"], AF.Exp,
                                                   scale=-0.5))
                        Sc.wait_ge(sch.h, s4)
                        s5 = ssm.inc(Sc.mul(ST["sstar"], ST["rs"], ST["r2"]))
                        V.wait_ge(ssm.h, s5)
                        if g is not None:
                            V.wait_ge(SM["dT2"].h, g["t2_wr"])
                        t2c = vsm.inc(V.tensor_scalar(t2tm[m][:, :], x1b[m][:, :],
                                                      ST["negm"], ST["sstar"],
                                                      OP.add, OP.mult))
                        Sy.wait_ge(vsm.h, t2c)
                        ins = Sy.dma_start(
                            t2_sc[256 * a + 128 * m: 256 * a + 128 * (m + 1), :],
                            t2tm[m][:, :])
                        SM["dT2"].inc(ins, 16)
                        fin_guard[(a, m)] = {"x1_wr": x1_wr,
                                             "t2_wr": SM["dT2"].n}
                return fin

            pending_fin = make_fin()
        pending_fin()
        pending_fin = None

        stage2_end = {k: SM[k].n for k in sem_names}

        tA_end = {}
        tB_end = {}
        vA_ps = {}
        vA_hw = {}
        vB = {}
        gas_c = {}

        def wload(L, wq_guard):
            eb = L % 2
            if L >= 2:
                G.wait_ge(tsm.h, wq_guard)
            else:
                for kk in ("tsm", "vsm", "ssm"):
                    G.wait_ge(SM[kk].h, stage2_end[kk])
            e = L % 32
            ins = G.dma_start((w1A if eb == 0 else w1B)[:, :], W1H[e])
            SM["dW"].inc(ins, 16)
            ins = G.dma_start((w2A if eb == 0 else w2B)[:, :], W2H[e])
            SM["dW"].inc(ins, 16)
            return SM["dW"].n

        wq_c = {0: wload(0, None), 1: wload(1, None)}
        tb = [t2T, pw_sb]

        def t2_load_scale(Q, guard_vch=None, guard_tsm=None):
            buf = tb[Q % 2]
            if guard_tsm is not None:
                Sc.wait_ge(tsm.h, guard_tsm)
            if guard_vch is not None:
                Sc.wait_ge(vch.h, guard_vch)
            Sc.wait_ge(SM["dT2"].h, stage2_end["dT2"])
            for dc in range(8):
                ins = Sc.dma_start(
                    buf[:, 1024 * dc: 1024 * (dc + 1)],
                    t2_sc[1024 * Q: 1024 * (Q + 1), 128 * dc: 128 * (dc + 1)],
                    transpose=True)
                SM["dT2L"].inc(ins, 16)
            Sc.wait_ge(SM["dT2L"].h, SM["dT2L"].n)
            Sc.wait_ge(dMISC.h, setup_cnt)
            for dc in range(8):
                ins = Sc.mul(buf[:, 1024 * dc: 1024 * (dc + 1)],
                             buf[:, 1024 * dc: 1024 * (dc + 1)],
                             nsc_sb[:, dc:dc + 1])
            return ssm.inc(ins)

        next_scale_c = t2_load_scale(0, guard_tsm=stage2_end["tsm"])
        prevQ = None
        for Q in range(4):
            L0 = 32 * Q
            act = tb[Q % 2]
            pwq = tb[(Q + 1) % 2]
            t2scale_c = next_scale_c
            if prevQ:
                Sy.wait_ge(SM["dOUT"].h, prevQ["out_wr"])
            Sy.wait_ge(SM["dX1"].h, stage2_end["dX1"])
            Sy.wait_ge(SM["dT2"].h, stage2_end["dT2"])
            Sy.wait_ge(ssm.h, stage2_end["ssm"])
            src_x1 = bass.AP(x1_sc.tensor, 1024 * Q * D,
                             [[D, 128], [128 * D, 8], [1, 1024]])
            ins = Sy.dma_start(oacc[:, :].rearrange("p (s n) -> p s n", s=8),
                               src_x1)
            SM["dOA"].inc(ins, 16)
            oa_c = SM["dOA"].n
            gate_e = {}
            for s in range(8):
                T.wait_ge(ssm.h, t2scale_c)
                if s > 0:
                    T.wait_ge(vsm.h, gate_e[s - 1])
                elif prevQ:
                    T.wait_ge(vsm.h, prevQ["lastA_v"])
                for dc in range(8):
                    ins = T.matmul(PS[:, 0:32],
                                   act[:, 1024 * dc + 128 * s: 1024 * dc + 128 * s + 128],
                                   gwT_sb[:, 32 * dc: 32 * (dc + 1)],
                                   start=(dc == 0), stop=(dc == 7))
                tg = tsm.inc(ins)
                V.wait_ge(tsm.h, tg)
                gate_e[s] = vsm.inc(V.tensor_tensor(g_sb[:, 32 * s:32 * (s + 1)],
                                                    PS[:, 0:32], gbrep[:, :],
                                                    OP.add))
            V.wait_ge(vsm.h, gate_e[7])
            if prevQ:
                V.wait_ge(tsm.h, prevQ["lastA_mm"])
            ap_j = bass.AP(g_sb, 0, [[256, 128], [32, 8], [1, 32], [0, 32]])
            ap_e = bass.AP(g_sb, 0, [[256, 128], [32, 8], [0, 32], [1, 32]])
            ap_pw_w = bass.AP(pwq, 0, [[8192, 128], [1024, 8], [32, 32], [1, 32]])
            ap_pw_rd = bass.AP(pwq, 0, [[8192, 128], [32, 256], [1, 32]])
            c1 = vch.inc(V.tensor_tensor(ap_pw_w, ap_j, ap_e, OP.is_lt))
            V.wait_ge(vch.h, c1)
            c2 = vch.inc(V.tensor_reduce(rnkf[:, :], ap_pw_rd, AX.X, OP.add))
            V.wait_ge(vch.h, c2)
            c3 = vch.inc(V.tensor_tensor(ap_pw_w, ap_j, ap_e, OP.is_equal))
            V.wait_ge(vch.h, c3)
            ap_tri = bass.AP(tri_sb, 0, [[1024, 128], [0, 8], [1, 1024]])
            ap_pw3 = bass.AP(pwq, 0, [[8192, 128], [1024, 8], [1, 1024]])
            c4 = vch.inc(V.tensor_tensor(ap_pw3, ap_pw3, ap_tri, OP.mult))
            V.wait_ge(vch.h, c4)
            c5 = vch.inc(V.tensor_reduce(msk_sb[:, :], ap_pw_rd, AX.X, OP.add))
            V.wait_ge(vch.h, c5)
            c6 = vch.inc(V.tensor_tensor(msk_sb[:, :], msk_sb[:, :], rnkf[:, :],
                                         OP.add))
            V.wait_ge(vch.h, c6)
            c7 = vch.inc(V.tensor_single_scalar(msk_sb[:, :], msk_sb[:, :], 3.5,
                                                OP.is_lt))
            ap_g2 = bass.AP(g_sb, 0, [[256, 128], [32, 8], [1, 32]])
            c8 = vch.inc(V.tensor_reduce(nmx[:, :], ap_g2, AX.X, OP.max,
                                         negate=True))
            Sc.wait_ge(vch.h, c8)
            for s in range(8):
                ins = Sc.activation(me_sb[:, 32 * s:32 * (s + 1)],
                                    g_sb[:, 32 * s:32 * (s + 1)], AF.Exp,
                                    bias=nmx[:, s:s + 1])
            sexp = ssm.inc(ins)
            V.wait_ge(ssm.h, sexp)
            V.wait_ge(vch.h, c7)
            c9 = vch.inc(V.tensor_tensor(me_sb[:, :], me_sb[:, :], msk_sb[:, :],
                                         OP.mult))
            V.wait_ge(vch.h, c9)
            ap_me2 = bass.AP(me_sb, 0, [[256, 128], [32, 8], [1, 32]])
            c10 = vch.inc(V.tensor_reduce(zr[:, :], ap_me2, AX.X, OP.add))
            V.wait_ge(vch.h, c10)
            c11 = vch.inc(V.reciprocal(izr[:, :], zr[:, :]))
            V.wait_ge(vch.h, c11)
            ap_izr = bass.AP(izr, 0, [[8, 128], [1, 8], [0, 32]])
            cmb_c = vsm.inc(V.tensor_tensor(cmb[:, :], me_sb[:, :], ap_izr,
                                            OP.mult))
            if Q < 3:
                next_scale_c = t2_load_scale(
                    Q + 1, guard_vch=c5,
                    guard_tsm=prevQ["lastA_mm"] if prevQ else None)
            def emit_b2_block(cmb_c=cmb_c, Q=Q, oa_c=oa_c, prevQ=prevQ):
                trev = {}
                for s in range(8):
                    T.wait_ge(vsm.h, cmb_c)
                    if prevQ and s == 0:
                        T.wait_ge(vsm.h, prevQ["lastB_v"])
                    if s >= 1:
                        T.wait_ge(vsm.h, trev[s - 1])
                    tt = tsm.inc(T.transpose(
                        PS[0:32, 2048 + 128 * s: 2048 + 128 * (s + 1)],
                        cmb[:, 32 * s: 32 * (s + 1)], idf_sb[:, :]))
                    V.wait_ge(tsm.h, tt)
                    trev[s] = vsm.inc(V.tensor_copy(
                        cmbT[:, 128 * s: 128 * (s + 1)],
                        PS[0:32, 2048 + 128 * s: 2048 + 128 * (s + 1)]))
                for s in range(8):
                    T.wait_ge(vsm.h, trev[7])
                    if s >= 2:
                        T.wait_ge(vsm.h, vB[("b2", Q, s - 2)])
                    for hh in range(2):
                        ins = T.matmul(PS[:, 2048 + 1024 * (s % 2) + 512 * hh:
                                          2048 + 1024 * (s % 2) + 512 * (hh + 1)],
                                       cmbT[:, 128 * s: 128 * (s + 1)],
                                       b2_sb[:, 512 * hh: 512 * (hh + 1)],
                                       start=True, stop=True)
                    bt = tsm.inc(ins)
                    V.wait_ge(tsm.h, bt)
                    if s == 0:
                        V.wait_ge(SM["dOA"].h, oa_c)
                    b2add = V.tensor_tensor(
                        oacc[:, 1024 * s: 1024 * (s + 1)],
                        oacc[:, 1024 * s: 1024 * (s + 1)],
                        PS[:, 2048 + 1024 * (s % 2): 3072 + 1024 * (s % 2)],
                        OP.add)
                    vB[("b2", Q, s)] = vsm.inc(b2add)
            lastA_v = None
            for e in range(32):
                L = L0 + e
                eb = L % 2
                w1 = w1A if eb == 0 else w1B
                w2 = w2A if eb == 0 else w2B
                T.wait_ge(SM["dW"].h, wq_c[L])
                if e == 0:
                    T.wait_ge(vsm.h, gate_e[7])
                def emit_hw(Lu):
                    Lp, up = Lu
                    fcp, thp = up % 8, up // 8
                    V.wait_ge(ssm.h, gas_c[Lu])
                    ins = V.tensor_tensor(
                        hwT[:, 1024 * fcp + 512 * thp:
                            1024 * fcp + 512 * (thp + 1)],
                        gas[up % 2][:, :], lc2[up % 2][:, :], OP.mult)
                    vA_hw[Lu] = vsm.inc(ins)

                for u in range(16):
                    fc, th = u % 8, u // 8
                    ub = u % 2
                    if u >= 2:
                        T.wait_ge(vsm.h, vA_ps[(L, u - 2)])
                    elif L > L0:
                        T.wait_ge(vsm.h, vA_ps[(L - 1, 14 + u)])
                    elif prevQ:
                        T.wait_ge(vsm.h, prevQ["lastA_v"])
                    for d in range(8):
                        lg = w1[:, 1024 * d + 128 * fc: 1024 * d + 128 * fc + 128]
                        ll = w1[:, 8192 + 1024 * d + 128 * fc:
                                8192 + 1024 * d + 128 * fc + 128]
                        rhs = act[:, 1024 * d + 512 * th: 1024 * d + 512 * (th + 1)]
                        T.matmul(bank(2 * ub), lg, rhs, start=(d == 0),
                                 stop=(d == 7))
                        ins = T.matmul(bank(2 * ub + 1), ll, rhs,
                                       start=(d == 0), stop=(d == 7))
                    tA_end[(L, u)] = tsm.inc(ins)
                    V.wait_ge(tsm.h, tA_end[(L, u)])
                    gprev = gas_c.get((L, u - 2) if u > 1 else (L - 1, 14 + u))
                    if gprev is not None:
                        V.wait_ge(ssm.h, gprev)
                    v1 = vsm.inc(V.tensor_scalar(gmin[ub][:, :], bank(2 * ub),
                                                 b1gb[:, 8 * e + fc: 8 * e + fc + 1],
                                                 7.0, OP.add, OP.min))
                    vsm.inc(V.tensor_scalar(lc1[:, :], bank(2 * ub + 1),
                                            b1lb[:, 8 * e + fc: 8 * e + fc + 1],
                                            7.0, OP.add, OP.min))
                    vA_ps[(L, u)] = vsm.n
                    vsm.inc(V.tensor_scalar(lc2[ub][:, :], lc1[:, :], -7.0, 1.0,
                                            OP.max, OP.add))
                    Sc.wait_ge(vsm.h, v1)
                    gas_c[(L, u)] = ssm.inc(
                        Sc.activation(gas[ub][:, :], gmin[ub][:, :],
                                      AF.Gelu_apprx_sigmoid))
                    prev_u = (L, u - 1) if u > 0 else (L - 1, 15)
                    if prev_u in gas_c:
                        emit_hw(prev_u)
                    if e == 0 and u == 8:
                        emit_b2_block()
                emit_hw((L, 15))
                lastA_mm = tA_end[(L, 15)]
                lastA_v = vA_hw[(L, 15)]
                for s in range(8):
                    ob = s % 2
                    T.wait_ge(vsm.h, vA_hw[(L, 7 if s < 4 else 15)])
                    if s >= 2:
                        T.wait_ge(vsm.h, vB[(L, s - 2)])
                    elif L > 0 and (L - 1, 6 + s) in vB:
                        T.wait_ge(vsm.h, vB[(L - 1, 6 + s)])
                    if e == 0:
                        T.wait_ge(vsm.h, vB[("b2", Q, s)])
                    for f in range(8):
                        lhsT = hwT[:, 1024 * f + 128 * s: 1024 * f + 128 * s + 128]
                        for hh in range(2):
                            ins = T.matmul(PS[:, 2048 + 1024 * ob + 512 * hh:
                                              2048 + 1024 * ob + 512 * (hh + 1)],
                                           lhsT,
                                           w2[:, 1024 * f + 512 * hh:
                                              1024 * f + 512 * (hh + 1)],
                                           start=(f == 0), stop=(f == 7))
                    tB_end[(L, s)] = tsm.inc(ins)
                    V.wait_ge(tsm.h, tB_end[(L, s)])
                    if e == 0:
                        V.wait_ge(SM["dOA"].h, oa_c)
                    ins = V.scalar_tensor_tensor(
                        oacc[:, 1024 * s: 1024 * (s + 1)],
                        PS[:, 2048 + 1024 * ob: 3072 + 1024 * ob],
                        cmb[:, 32 * s + e: 32 * s + e + 1],
                        oacc[:, 1024 * s: 1024 * (s + 1)],
                        OP.mult, OP.add)
                    vB[(L, s)] = vsm.inc(ins)
                tB_end[L] = tsm.n
                if L + 2 < 128:
                    wq_c[L + 2] = wload(L + 2, tB_end[L])
            for s in range(8):
                Sy.wait_ge(vsm.h, vB[(L0 + 31, s)])
                ins = Sy.dma_start(
                    out[1024 * Q + 128 * s: 1024 * Q + 128 * (s + 1), :],
                    oacc[:, 1024 * s: 1024 * (s + 1)])
                SM["dOUT"].inc(ins, 16)
            prevQ = {"lastA_mm": lastA_mm, "lastA_v": lastA_v,
                     "tB_last": tB_end[L0 + 31], "out_wr": SM["dOUT"].n,
                     "lastB_v": vB[(L0 + 31, 7)]}

        Sy.wait_ge(SM["dOUT"].h, SM["dOUT"].n)

    stack.close()
    return nc


_NC_CACHE = None


def kernel(L0_attn_x, L0_attn_W_qkv, L0_ffn_norm_scale, L0_ffn_gate_w,
           L0_ffn_gate_b, L0_ffn_w1, L0_ffn_b1, L0_ffn_w2, L0_ffn_b2):
    global _NC_CACHE
    x = np.asarray(L0_attn_x, np.float32)
    Wqkv = np.ascontiguousarray(np.asarray(L0_attn_W_qkv, np.float32))
    norm_scale = np.asarray(L0_ffn_norm_scale, np.float32)
    gate_w = np.asarray(L0_ffn_gate_w, np.float32)
    gate_b = np.asarray(L0_ffn_gate_b, np.float32)
    w1 = np.asarray(L0_ffn_w1, np.float32)
    b1 = np.asarray(L0_ffn_b1, np.float32)
    w2 = np.asarray(L0_ffn_w2, np.float32)
    b2 = np.asarray(L0_ffn_b2, np.float32)

    wkv_h = np.ascontiguousarray(Wqkv[:, 1024:3072])
    wq_h = np.ascontiguousarray(Wqkv[:, 0:1024])
    gwT_h = np.ascontiguousarray(gate_w.T)
    gbr_h = np.ascontiguousarray(np.broadcast_to(gate_b.reshape(1, 32), (128, 32)))
    nsc_h = np.ascontiguousarray(norm_scale.reshape(8, 128).T)
    b1g_h = np.ascontiguousarray(
        b1[:, 0::2].reshape(32, 8, 128).transpose(2, 0, 1).reshape(128, 256))
    b1l_h = np.ascontiguousarray(
        b1[:, 1::2].reshape(32, 8, 128).transpose(2, 0, 1).reshape(128, 256))
    w1g = w1[:, 0::2, :].transpose(0, 2, 1)
    w1l = w1[:, 1::2, :].transpose(0, 2, 1)
    def w1_re(a):
        return (a.reshape(32, 8, 128, 8, 128).transpose(0, 2, 1, 3, 4)
                .reshape(32, 128, 8192))
    W1H_h = np.ascontiguousarray(
        np.concatenate([w1_re(w1g), w1_re(w1l)], axis=2))
    W2H_h = np.ascontiguousarray(
        w2.transpose(0, 2, 1).reshape(32, 8, 128, 1024)
        .transpose(0, 2, 1, 3).reshape(32, 128, 8192))
    b2m_h = np.ascontiguousarray(b2)
    onesr_h = np.ones((1, 128), ml_dtypes.bfloat16)
    onescol_h = np.ones((128, 1), ml_dtypes.bfloat16)
    trim_h = np.ascontiguousarray(np.broadcast_to(
        (np.arange(32)[None, :] < np.arange(32)[:, None]).astype(
            ml_dtypes.bfloat16).reshape(1, 1024), (128, 1024)))

    in_maps = []
    for c in range(8):
        seq, sub = c // 4, c % 4
        xs = x[seq]
        xq_h = np.ascontiguousarray(
            xs.reshape(128, 128, 1024)[sub::4].reshape(4096, 1024))
        m = np.zeros((8, 128, 128), np.float32)
        for j in range(8):
            d = j - sub
            if d == 0:
                p = np.arange(128)
                m[j] = np.where(p[:, None] > p[None, :], NEG, 0.0)
            elif d > 0:
                m[j] = NEG
        in_maps.append({
            "xk": np.ascontiguousarray(xs), "xq": xq_h,
            "wkv": wkv_h, "wq": wq_h, "gwT": gwT_h, "gbr": gbr_h,
            "nsc": nsc_h, "b1g": b1g_h, "b1l": b1l_h, "b2m": b2m_h,
            "W1H": W1H_h, "W2H": W2H_h, "masks": m, "trim": trim_h,
            "onesr": onesr_h, "onescol": onescol_h,
            "identf": np.eye(128, dtype=np.float32),
        })

    if _NC_CACHE is None:
        _NC_CACHE = build_program()
    res = run_bass_kernel_spmd(_NC_CACHE, in_maps, core_ids=list(range(8)))

    out_full = np.empty((2, 16384, 1024), np.float32)
    for c in range(8):
        seq, sub = c // 4, c % 4
        view = out_full[seq].reshape(128, 128, 1024)
        view[sub::4] = res.results[c]["out"].reshape(32, 128, 1024)
    return out_full
